# version 05
# baseline (speedup 1.0000x reference)
import numpy as np
import ml_dtypes
from contextlib import ExitStack

import concourse.bass as bass
import concourse.mybir as mybir
from concourse.bass_utils import run_bass_kernel_spmd

F32 = mybir.dt.float32
BF16 = mybir.dt.bfloat16
AF = mybir.ActivationFunctionType
ALU = mybir.AluOpType
AX = mybir.AxisListType
BFNP = ml_dtypes.bfloat16

D = 1024
T = 4096
TC = 2048
NKC = 8
DH = 64
NF = 16
EPS_LN = 1e-5
EPS_LN1 = 1e-6
EPS_RMS = float(np.finfo(np.float32).eps)

N_STAGES = 5


def _finish(P, nc, out_toks, debug_taps, tapmap):
    if debug_taps:
        for nm in debug_taps:
            if nm not in tapmap:
                continue
            ap, shp, dt_ = tapmap[nm]
            dbg = nc.dram_tensor("dbg_" + nm, shp, dt_, kind="ExternalOutput").ap()
            P.waits("sync", list(out_toks))
            out_toks.append(P.dma("sync", dbg, ap))
    P.waits("sync", out_toks)


class Prog:
    ENGS = ("sync", "tensor", "vector", "scalar", "gpsimd")

    ROT = ("LD", "WTS", "XS", "ACC", "CS", "ST")

    def __init__(self, nc):
        self.nc = nc
        self.ops = {e: [] for e in self.ENGS}
        self.sems = {}
        self.counts = {}
        for name in ("LD", "WD", "TE", "VE", "SE", "GE", "CC",
                     "CO", "CS", "WTS", "XS", "ACC", "ST", "AGD"):
            self.sems[name] = nc.alloc_semaphore(name)
            self.counts[name] = 0
        for name in self.ROT:
            self.sems[name + "B"] = nc.alloc_semaphore(name + "B")
            self.counts[name + "B"] = 0
        self.rotctr = {name: 0 for name in self.ROT}

    def _inc(self, name, amt):
        self.counts[name] += amt
        return (self.sems[name], self.counts[name])

    def wait(self, eng, token):
        self.waits(eng, [token])

    def waits(self, eng, tokens):
        best = {}
        flat = []
        for t in tokens:
            if isinstance(t, list):
                flat.extend(t)
            else:
                flat.append(t)
        for t in flat:
            if t is None:
                continue
            s, v = t
            k = id(s)
            if k not in best or v > best[k][1]:
                best[k] = (s, v)
        for s, v in best.values():
            if v > 0:
                self.ops[eng].append(lambda e, s=s, v=v: e.wait_ge(s, v))

    def dma(self, queue, out_ap, in_ap, sem=None, **kw):
        base = sem or ("LD" if queue == "sync" else "WD")
        if base in self.ROT:
            self.rotctr[base] ^= 1
            phys = base if self.rotctr[base] == 0 else base + "B"
            other = base + "B" if phys == base else base
        else:
            phys, other = base, None
        sem = self.sems[phys]
        prev = self.counts[phys]
        if prev > 0:
            self.waits(queue, [(sem, prev)])

        def f(e, o=out_ap, i=in_ap, kw=kw, s=sem):
            e.dma_start(out=o, in_=i, **kw).then_inc(s, 16)
        self.ops[queue].append(f)
        tok = self._inc(phys, 16)
        if other is not None and self.counts[other] > 0:
            return [tok, (self.sems[other], self.counts[other])]
        return tok

    def op(self, eng, fn, inc=None):
        if inc:
            sem = self.sems[inc]

            def g(e, fn=fn, sem=sem):
                fn(e).then_inc(sem, 1)
            self.ops[eng].append(g)
            return self._inc(inc, 1)
        self.ops[eng].append(lambda e, fn=fn: fn(e))
        return None

    def opseq(self, eng, fns, inc):
        tok = None
        for i, fn in enumerate(fns):
            if i > 0:
                self.waits(eng, [tok])
            tok = self.op(eng, fn, inc=inc)
        return tok

    def run(self, blk):
        for eng_name in self.ENGS:
            thunks = list(self.ops[eng_name])

            def make(th):
                def body(eng):
                    for f in th:
                        f(eng)
                return body
            getattr(blk, eng_name)(make(thunks))


DEBUG_TAPS = ()


def build_program(n_stages=N_STAGES, debug_taps=None):
    if debug_taps is None:
        debug_taps = DEBUG_TAPS
    nc = bass.Bass("TRN2", target_bir_lowering=False)

    def din(name, shape, dtype=F32):
        return nc.dram_tensor(name, list(shape), dtype, kind="ExternalInput").ap()

    xT = din("xT", [D, T])
    wqkvT = din("wqkvT", [D, 3 * D])
    wfT = din("wfT", [2 * NF, DH])
    bfp = din("bfp", [128, 1])
    woutT = din("woutT", [D, D])
    bout = din("bout", [128, 8])
    gwT = din("gwT", [D, 8])
    w1 = din("w1", [8, D, D])
    w2 = din("w2", [8, D, D])
    w3 = din("w3", [8, D, D])
    w1s = din("w1s", [D, D])
    w2s = din("w2s", [D, D])
    w3s = din("w3s", [D, D])
    Wp = din("Wp", [D, 2 * D])
    bp = din("bp", [128, 16])
    gam = din("gam", [128, 8])
    bet = din("bet", [128, 8])
    gsh = din("gsh", [128, 8])
    grt = din("grt", [128, 8])
    W1s_i = din("W1s", [2, D, 4 * D])
    b1s_i = din("b1s", [128, 2, 32])
    W2s_i = din("W2s", [2, 4 * D, D])
    b2s_i = din("b2s", [128, 2, 8])
    W1r_i = din("W1r", [8, D, 4 * D])
    b1r_i = din("b1r", [128, 8, 32])
    W2r_i = din("W2r", [8, 4 * D, D])
    b2r_i = din("b2r", [128, 8, 8])
    cent = din("cent", [D, 8])
    cosT = din("cosT", [128, T])
    sinT = din("sinT", [128, T])
    fourT = din("fourT", [2 * NF, T])
    R2 = din("R2", [128, 128])
    invpar = din("invpar", [128, 1])
    sel2 = din("sel2", [2, 1])

    out = nc.dram_tensor("out", [D, TC], F32, kind="ExternalOutput").ap()

    xn0_d = nc.dram_tensor("xn0_d", [D, T], BF16, kind="Internal").ap()
    xn_d = nc.dram_tensor("xn_d", [D, TC], BF16, kind="Internal").ap()
    ug_d = nc.dram_tensor("ug_d", [D, TC], BF16, kind="Internal").ap()
    rx_d = nc.dram_tensor("rx_d", [D, TC], F32, kind="Internal").ap()
    dl_d = nc.dram_tensor("dl_d", [D, TC], F32, kind="Internal").ap()
    ag_in = nc.dram_tensor("ag_in", [128, 8], F32, kind="Internal").ap()
    ag_out = nc.dram_tensor("ag_out", [256, 8], F32, kind="Internal").ap()
    pr_d = nc.dram_tensor("pr_d", [1, D], F32, kind="Internal").ap()

    es = ExitStack()

    def sb(name, shape, dtype):
        return es.enter_context(nc.sbuf_tensor(name, list(shape), dtype))

    def psum_t(name, shape, dtype=F32):
        return es.enter_context(nc.psum_tensor(name, list(shape), dtype))

    ones_col = sb("ones_col", [128, 1], BF16)
    ones_row = sb("ones_row", [1, 128], BF16)
    identf = sb("identf", [128, 128], F32)
    ind0 = sb("ind0", [128, 1], BF16)
    ind1 = sb("ind1", [128, 1], BF16)
    zeros_t = sb("zeros_t", [128, 512], F32)
    invp_sb = sb("invp_sb", [128, 1], F32)
    sel_sb = sb("sel_sb", [2, 1], F32)
    bfp_c = sb("bfp_c", [128, 1], F32)
    bout_c = sb("bout_c", [128, 8], F32)
    bp_c = sb("bp_c", [128, 16], F32)
    bpn_c = sb("bpn_c", [128, 16], F32)
    gam_c = sb("gam_c", [128, 8], F32)
    bet_c = sb("bet_c", [128, 8], F32)
    gsh_c = sb("gsh_c", [128, 8], F32)
    grt_c = sb("grt_c", [128, 8], F32)
    b1s_c = sb("b1s_c", [128, 2, 32], F32)
    b2s_c = sb("b2s_c", [128, 2, 8], F32)
    b1r_c = sb("b1r_c", [128, 8, 32], F32)
    b2r_c = sb("b2r_c", [128, 8, 8], F32)
    mcol = sb("mcol", [128, 8], F32)
    carry_c = sb("carry_c", [128, 8], F32)
    pP_c = sb("pP_c", [128, 8], F32)
    pcar_c = sb("pcar_c", [128, 8], F32)
    msc = sb("msc", [128, 16, 8], F32)
    gkc = sb("gkc", [128, 16, 8], F32)
    gwT_sb = sb("gwT_sb", [128, 8, 8], BF16)
    cent_sb = sb("cent_sb", [128, 8, 8], BF16)
    rowA = sb("rowA", [1, 1024], F32)
    rowB = sb("rowB", [1, 1024], F32)
    rowC = sb("rowC", [1, 1024], F32)
    rowAB = sb("rowAB", [1, 1024], BF16)
    rmsrow = sb("rmsrow", [1, 2048], F32)
    s16a = sb("s16a", [128, 2, 512], BF16)
    s16b = sb("s16b", [128, 2, 512], BF16)
    s16c = sb("s16c", [128, 2, 512], BF16)
    s32a = sb("s32a", [128, 2, 512], F32)
    s32b = sb("s32b", [128, 2, 512], F32)
    ptile = sb("ptile", [128, 2, 1024], BF16)
    arep = sb("arep", [128, 2, 512], BF16)
    brep = sb("brep", [128, 2, 512], BF16)
    r2sb = sb("r2sb", [128, 128], BF16)
    wfT_sb = sb("wfT_sb", [32, 64], BF16)

    psS = psum_t("psS", [128, 1024])
    psS2 = psum_t("psS2", [128, 1024])
    ps0 = psum_t("ps0", [128, 512])
    ps1 = psum_t("ps1", [128, 512])
    ps2 = psum_t("ps2", [128, 512])
    ps3 = psum_t("ps3", [128, 512])

    SBUF_TOTAL = 224 * 1024
    used = SBUF_TOTAL - nc.sbuf_bytes_remaining
    ARENA = (used + 63) & ~63
    LIMIT = SBUF_TOTAL - 512

    def at(name, shape, dtype, off_kb):
        off = ARENA + int(off_kb * 1024)
        nbytes = 1
        for s in shape[1:]:
            nbytes *= s
        nbytes *= mybir.dt.size(dtype)
        assert off + nbytes <= LIMIT, f"{name}: arena overflow ({off + nbytes} > {LIMIT})"
        return nc.alloc_sbuf_tensor_at(name, list(shape), dtype, offset=off)

    P = Prog(nc)
    rowab_war = [None]
    lnsq_war = [None]

    t_inv = P.dma("sync", invp_sb[:], invpar)
    t_sel = P.dma("gpsimd", sel_sb[:], sel2, sem="CO")
    t_r2 = P.dma("gpsimd", r2sb[:], R2, sem="CO")
    t_wf = P.dma("gpsimd", wfT_sb[:], wfT, sem="CO")
    t_gw = P.dma("gpsimd", gwT_sb[:], gwT.rearrange("(c p) n -> p c n", p=128), sem="CO")
    t_ce = P.dma("gpsimd", cent_sb[:], cent.rearrange("(c p) n -> p c n", p=128), sem="CO")

    t_idf0 = P.op("gpsimd", lambda e: e.memset(identf[:], 0.0), inc="GE")
    P.waits("gpsimd", [t_idf0])
    t_idf = P.op("gpsimd", lambda e: e.affine_select(
        out=identf[:], in_=identf[:], compare_op=ALU.not_equal, fill=1.0, base=0,
        pattern=[[-1, 128]], channel_multiplier=1), inc="GE")

    def c_init1(e):
        e.memset(ones_col[:], 1.0)
        e.memset(ones_row[:], 1.0)
        e.memset(ind0[:], 0.0)
        e.memset(ind1[:], 0.0)
        e.memset(zeros_t[:], 0.0)
        e.memset(carry_c[:], 0.0)
        return e.memset(pP_c[:], 1.0)

    def c_init2(e):
        e.memset(ind0[0:64, :], 1.0)
        return e.memset(ind1[64:128, :], 1.0)
    t_cst = P.opseq("vector", [c_init1, c_init2], inc="VE")
    P.wait("vector", t_idf)

    t_b = []
    for src_ap, dst in ((bfp, bfp_c[:]), (bout, bout_c[:]), (bp, bp_c[:]),
                        (gam, gam_c[:]), (bet, bet_c[:]), (gsh, gsh_c[:]),
                        (grt, grt_c[:]), (b1s_i, b1s_c[:]), (b2s_i, b2s_c[:]),
                        (b1r_i, b1r_c[:]), (b2r_i, b2r_c[:])):
        t_b.append(P.dma("sync", dst, src_ap))
    wt = t_b[-1]
    t_cols = wt
    P.wait("vector", t_cols)
    t_bpn = P.op("vector", lambda e: e.tensor_scalar(
        bpn_c[:], bp_c[:], -1.0, None, ALU.mult), inc="VE")

    def ln_rows(n_tok, eps, stats_tok, rms, par, save_rows=None):
        P.waits("vector", [stats_tok, lnsq_war[0]])
        if not rms:
            tv = P.opseq("vector", [
                lambda e, n=n_tok: (
                    e.tensor_scalar(rowA[0:1, 0:n], psS[0:1, 0:n], 1.0 / D, None, ALU.mult),
                    e.tensor_scalar(rowB[0:1, 0:n], psS[0:1, 512:512 + n], 1.0 / D, None, ALU.mult))[-1],
                lambda e, n=n_tok: e.tensor_mul(rowC[0:1, 0:n], rowA[0:1, 0:n], rowA[0:1, 0:n]),
                lambda e, n=n_tok: e.tensor_sub(rowB[0:1, 0:n], rowB[0:1, 0:n], rowC[0:1, 0:n]),
                lambda e, n=n_tok: e.tensor_scalar(rowB[0:1, 0:n], rowB[0:1, 0:n], float(eps), None, ALU.add),
            ], inc="VE")
        else:
            tv = P.opseq("vector", [
                lambda e, n=n_tok: e.tensor_scalar(rowB[0:1, 0:n], psS[0:1, 512:512 + n], 1.0 / D, None, ALU.mult),
                lambda e, n=n_tok: e.tensor_scalar(rowB[0:1, 0:n], rowB[0:1, 0:n], float(eps), None, ALU.add),
            ], inc="VE")
        P.wait("scalar", tv)
        ts = P.op("scalar", lambda e, n=n_tok: e.activation(
            rowC[0:1, 0:n], rowB[0:1, 0:n], AF.Sqrt), inc="SE")
        lnsq_war[0] = ts
        P.wait("vector", ts)

        def g1(e, n=n_tok):
            return e.reciprocal(rowB[0:1, 0:n], rowC[0:1, 0:n])

        def g2(e, n=n_tok):
            r = e.tensor_copy(rowAB[0:1, 0:n], rowB[0:1, 0:n])
            if not rms:
                r = e.tensor_mul(rowA[0:1, 0:n], rowA[0:1, 0:n], rowB[0:1, 0:n])
            if save_rows is not None:
                r = e.tensor_copy(save_rows[0], rowB[0:1, 0:n])
            return r

        def g3(e, n=n_tok):
            return e.tensor_scalar(rowA[0:1, 0:n], rowA[0:1, 0:n], -1.0, None, ALU.mult)

        def g4(e, n=n_tok):
            r = e.tensor_copy(rowAB[0:1, 512:512 + n], rowA[0:1, 0:n])
            if save_rows is not None and len(save_rows) > 1:
                r = e.tensor_copy(save_rows[1], rowA[0:1, 0:n])
            return r
        P.waits("vector", [rowab_war[0]])
        groups = [g1, g2] if rms else [g1, g2, g3, g4]
        tv2 = P.opseq("vector", groups, inc="VE")
        P.wait("tensor", tv2)

        def tf(e, n=n_tok):
            r = e.matmul(ps2[:, 0:n], ones_row[:], rowAB[0:1, 0:n], start=True, stop=True)
            if not rms:
                r = e.matmul(ps3[:, 0:n], ones_row[:], rowAB[0:1, 512:512 + n],
                             start=True, stop=True)
            return r
        tt = P.op("tensor", tf, inc="TE")
        rowab_war[0] = tt
        P.wait("vector", tt)

        def vf3(e, n=n_tok, p=par):
            r = e.tensor_copy(arep[:, p, 0:n], ps2[:, 0:n])
            if not rms:
                r = e.tensor_copy(brep[:, p, 0:n], ps3[:, 0:n])
            return r
        return P.op("vector", vf3, inc="VE")

    def stats_mms(slab_ap_fn, par, sq_buf, pre_toks, rms=False):
        st_tok = None
        for kc in range(NKC):
            P.waits("scalar", pre_toks if kc == 0 else [st_tok])
            tsq = P.op("scalar", lambda e, kc=kc: e.activation(
                sq_buf[:, kc % 2, :], slab_ap_fn(kc), AF.Square), inc="SE")
            P.waits("tensor", [tsq, t_cst])

            def tf(e, kc=kc):
                if not rms:
                    e.matmul(psS[0:1, 0:512], ones_col[:], slab_ap_fn(kc),
                             start=(kc == 0), stop=(kc == NKC - 1))
                return e.matmul(psS[0:1, 512:1024], ones_col[:], sq_buf[:, kc % 2, :],
                                start=(kc == 0), stop=(kc == NKC - 1))
            st_tok = P.op("tensor", tf, inc="TE")
        return st_tok

    xslab = at("xslab", [128, 2, NKC, 512], BF16, 0)
    xnslab = at("xnslab", [128, 2, NKC, 512], BF16, 16)
    sqs = at("sqs", [128, 2, 512], BF16, 32)

    slab_war = [None, None]
    xn_war = [None, None]
    ln0_st = []
    for tb in range(8):
        par = tb % 2
        P.waits("gpsimd", [slab_war[par]])
        tk_ld = P.dma("gpsimd", xslab[:, par],
                      xT[:, tb * 512:(tb + 1) * 512].rearrange("(c p) n -> p c n", p=128),
                      sem="XS")
        st_tok = stats_mms(lambda kc, p=par: xslab[:, p, kc, :], par, sqs,
                           [tk_ld, slab_war[par]])
        t_ab = ln_rows(512, EPS_LN, st_tok, rms=False, par=par)
        P.waits("vector", [t_ab, xn_war[par]])

        t_app = P.opseq("vector", [
            lambda e, p=par: [e.tensor_mul(xnslab[:, p, kc, :], xslab[:, p, kc, :],
                                           arep[:, p, 0:512]) for kc in range(NKC)][-1],
            lambda e, p=par: [e.tensor_add(xnslab[:, p, kc, :], xnslab[:, p, kc, :],
                                           brep[:, p, 0:512]) for kc in range(NKC)][-1],
        ], inc="VE")
        slab_war[par] = t_app
        P.wait("sync", t_app)
        t_st = P.dma("sync", xn0_d[:, tb * 512:(tb + 1) * 512]
                     .rearrange("(c p) n -> p c n", p=128), xnslab[:, par])
        xn_war[par] = t_st
        ln0_st.append(t_st)

    s1_done = ln0_st[-1]
    if n_stages < 2:
        _finish(P, nc, list(ln0_st), debug_taps,
                {"xn0": (xn0_d, [D, T], BF16)})
        with nc.Block() as blk:
            P.run(blk)
        es.close()
        return nc

    kbuf = at("kbuf", [128, 4, T], BF16, 36)
    vbuf = at("vbuf", [128, 32, 8, 65], BF16, 68)
    qbuf = at("qbuf", [128, 4, TC], BF16, 102)
    pvqb = at("pvqb", [128, 4, 512], BF16, 26)
    wqkv_sb = at("wqkv_sb", [128, NKC, 1536], BF16, 118)
    costile = at("costile", [128, 4, 512], BF16, 142)
    sintile = at("sintile", [128, 4, 512], BF16, 146)
    woutp = at("woutp", [128, 2, NKC, 128], BF16, 150)
    fenc2 = at("fenc2", [128, T], BF16, 18)
    nslab = at("nslab", [128, 2, NKC, 512], BF16, 0)
    four_t = at("four_t", [32, 2, 512], BF16, 16)

    prev_tok = None
    P.waits("gpsimd", [xn_war[0], xn_war[1], s1_done, slab_war[0], slab_war[1]])
    ft_war = [None, None]
    for tb in range(8):
        par = tb % 2
        P.waits("gpsimd", [ft_war[par]])
        tkf = P.dma("gpsimd", four_t[:, par, :], fourT[:, tb * 512:(tb + 1) * 512], sem="CS")
        P.waits("tensor", [tkf, t_wf, prev_tok, t_cols])
        def tmf_f(e, p=par):
            e.matmul(ps0[0:64, :], wfT_sb[:], four_t[:, p, :], start=True, stop=True)
            return e.matmul(ps0[64:128, :], wfT_sb[:], four_t[:, p, :], start=True, stop=True)
        tmf = P.op("tensor", tmf_f, inc="TE")
        ft_war[par] = tmf
        P.wait("vector", tmf)

        def sff(e, tb=tb):
            return e.tensor_scalar(fenc2[:, tb * 512:(tb + 1) * 512], ps0[:, :],
                                   bfp_c[:, 0:1], None, ALU.add)
        prev_tok = P.op("vector", sff, inc="VE")
    t_fenc = prev_tok
    if debug_taps and "s2dbg" in debug_taps:
        dfe = nc.dram_tensor("dbg_fenc_early", [128, T], BF16, kind="ExternalOutput").ap()
        P.waits("sync", [t_fenc])
        P.dma("sync", dfe, fenc2[:])

    def load_phase_w(ph, after=()):
        toks = list(after)
        res = []
        for j, c0 in enumerate((ph * 512, D + ph * 512, 2 * D + ph * 512)):
            P.waits("gpsimd", toks)
            res.append(P.dma("gpsimd", wqkv_sb[:, :, j * 512:(j + 1) * 512],
                             wqkvT[:, c0:c0 + 512].rearrange("(c p) n -> p c n", p=128),
                             sem="WTS"))
        return res[-1]

    t_wph = load_phase_w(0)

    rope_war = None
    att_done_tok = None
    wout_war = [None, None]
    rxdma_toks = []

    rx_ph0_done = None
    dbg_kpre = None
    dbg_kpre_tok = [None]
    if debug_taps and "s2kpre" in debug_taps:
        dbg_kpre = nc.dram_tensor("dbg_kpre", [128, T], BF16, kind="ExternalOutput").ap()
        dbg_rot = nc.dram_tensor("dbg_rot", [128, T], F32, kind="ExternalOutput").ap()
        dbg_st = nc.dram_tensor("dbg_st", [128, 4, 512], BF16, kind="ExternalOutput").ap()
    for ph in range(2):
        wtok = t_wph
        if ph == 1:
            P.waits("gpsimd", [rx_ph0_done])
        nslab_war = [None, None]
        cs_war = [None, None, None, None]
        last_rope = None
        vev_tok = None
        for tb in range(8):
            par = tb % 2
            par4 = tb % 4
            P.waits("gpsimd", [nslab_war[par], s1_done])
            tk_ns = P.dma("gpsimd", nslab[:, par],
                          xn0_d[:, tb * 512:(tb + 1) * 512].rearrange("(c p) n -> p c n", p=128),
                          sem="XS")
            P.waits("gpsimd", [cs_war[par4]])
            tk_cs = P.dma("gpsimd", costile[:, par4, :], cosT[:, tb * 512:(tb + 1) * 512], sem="CS")
            tk_sn = P.dma("gpsimd", sintile[:, par4, :], sinT[:, tb * 512:(tb + 1) * 512], sem="CS")
            is_q = tb < 4

            def emit_rope(oc, wcol0, dstbuf, dst_tb):
                nonlocal last_rope

                def tf_p(e, p=par, oc=oc, w0=wcol0):
                    r = None
                    for kc in range(NKC):
                        r = e.matmul((ps0 if oc % 2 == 0 else ps1)[:],
                                     wqkv_sb[:, kc, w0 + oc * 128:w0 + (oc + 1) * 128],
                                     nslab[:, p, kc, :],
                                     start=(kc == 0), stop=(kc == NKC - 1))
                    return r
                P.waits("tensor", [tk_ns, wtok, att_done_tok, last_rope, t_fenc])
                t_mm = P.op("tensor", tf_p, inc="TE")
                P.waits("scalar", [t_mm, last_rope])
                t_pre = P.op("scalar", lambda e, p=par, oc=oc: e.activation(
                    s16a[:, p, :], (ps0 if oc % 2 == 0 else ps1)[:], AF.Copy), inc="SE")
                if dbg_kpre is not None and ph == 0 and dstbuf is kbuf and oc == 0:
                    P.waits("sync", [t_pre])
                    dbg_kpre_tok[0] = P.dma("sync", dbg_kpre[:, tb * 512:(tb + 1) * 512],
                                            s16a[:, par, :])
                    P.waits("scalar", [dbg_kpre_tok[0]])
                    P.waits("tensor", [dbg_kpre_tok[0]])
                P.waits("tensor", [t_pre, t_r2])
                t_rot = P.op("tensor", lambda e, p=par: e.matmul(
                    psS2[:, 0:512], r2sb[:], s16a[:, p, :], start=True, stop=True), inc="TE")
                if dbg_kpre is not None and ph == 0 and dstbuf is kbuf and oc == 0:
                    P.waits("vector", [t_rot])
                    tvx = P.op("vector", lambda e: e.tensor_copy(s32a[:, 0, :], psS2[:, 0:512]), inc="VE")
                    P.waits("sync", [tvx])
                    tdx = P.dma("sync", dbg_rot[:, tb * 512:(tb + 1) * 512], s32a[:, 0, :])
                    P.waits("vector", [tdx])
                    P.waits("tensor", [tdx])
                P.waits("vector", [t_rot, tk_cs, tk_sn, t_fenc])

                last_rope = P.opseq("vector", [
                    lambda e, p=par, p4=par4: (
                        e.tensor_mul(s16b[:, p, :], s16a[:, p, :], costile[:, p4, :]),
                        e.tensor_copy(s16c[:, p, :], psS2[:, 0:512]))[-1],
                    lambda e, p=par, p4=par4: e.tensor_mul(s16c[:, p, :], s16c[:, p, :],
                                                           sintile[:, p4, :]),
                    lambda e, p=par, oc=oc, dst=dstbuf, dtb=dst_tb: e.tensor_add(
                        dst[:, oc, dtb * 512:(dtb + 1) * 512], s16b[:, p, :], s16c[:, p, :]),
                    lambda e, p=par, oc=oc, dst=dstbuf, dtb=dst_tb, tb=tb: e.tensor_add(
                        dst[:, oc, dtb * 512:(dtb + 1) * 512],
                        dst[:, oc, dtb * 512:(dtb + 1) * 512],
                        fenc2[:, tb * 512:(tb + 1) * 512]),
                ], inc="VE")
                P.waits("tensor", [last_rope])
                P.waits("scalar", [last_rope])
                P.waits("gpsimd", [last_rope])
                return last_rope

            for oc in range(4):
                emit_rope(oc, 512, kbuf, tb)
                if is_q:
                    emit_rope(oc, 0, qbuf, tb)
            for tt in range(4):
                gt = tb * 4 + tt
                P.waits("tensor", [tk_ns, wtok, att_done_tok, vev_tok])

                def tf_v(e, p=par, tt=tt):
                    r = None
                    for kc in range(NKC):
                        r = e.matmul(ps2[:],
                                     nslab[:, p, kc, tt * 128:(tt + 1) * 128],
                                     wqkv_sb[:, kc, 1024:1536],
                                     start=(kc == 0), stop=(kc == NKC - 1))
                    return r
                tv_mm = P.op("tensor", tf_v, inc="TE")
                P.wait("vector", tv_mm)

                def vf_v(e, gt=gt):
                    e.tensor_copy(vbuf[:, gt, :, 0:64],
                                  ps2[:].rearrange("p (h d) -> p h d", d=64))
                    return e.memset(vbuf[:, gt, :, 64:65], 1.0)
                vev_tok = P.op("vector", vf_v, inc="VE")
            nslab_war[par] = vev_tok if not is_q else last_rope
            nslab_war[par] = last_rope if last_rope else vev_tok
            if vev_tok is not None and last_rope is not None:
                nslab_war[par] = max([last_rope, vev_tok], key=lambda t: t[1])
            cs_war[par4] = last_rope

        t_qkv = nslab_war[0] if nslab_war[0][1] > nslab_war[1][1] else nslab_war[1]

        if ph == 0:
            P.waits("gpsimd", [t_qkv])
            t_wph = load_phase_w(1)

        nrm_v = t_qkv
        nrm_war = None
        for (buf, ntb, base_idx) in ((qbuf, 4, 0), (kbuf, 8, 16)):
            for oc in range(4):
                for tb in range(ntb):
                    P.waits("vector", [nrm_v, rope_war])
                    tsq = P.op("vector", lambda e, b=buf, oc=oc, tb=tb: e.tensor_mul(
                        s16a[:, tb % 2, :], b[:, oc, tb * 512:(tb + 1) * 512],
                        b[:, oc, tb * 512:(tb + 1) * 512]), inc="VE")
                    P.waits("tensor", [tsq, nrm_war])

                    def tf_n(e, tb=tb):
                        e.matmul(ps0[0:1, :], ind0[:], s16a[:, tb % 2, :], start=True, stop=True)
                        return e.matmul(ps1[0:1, :], ind1[:], s16a[:, tb % 2, :], start=True, stop=True)
                    tn = P.op("tensor", tf_n, inc="TE")
                    P.wait("vector", tn)

                    def vf_n1(e):
                        e.reduce_max(rowC[0:1, 100:101], ps0[0:1, :], AX.X)
                        return e.reduce_max(rowC[0:1, 101:102], ps1[0:1, :], AX.X)
                    t_n1 = P.op("vector", vf_n1, inc="VE")
                    P.wait("vector", t_n1)

                    def vf_n2(e, oc=oc, tb=tb, base=base_idx):
                        i0 = base + 2 * oc
                        if tb == 0:
                            e.tensor_copy(rowB[0:1, i0:i0 + 1], rowC[0:1, 100:101])
                            return e.tensor_copy(rowB[0:1, i0 + 1:i0 + 2], rowC[0:1, 101:102])
                        e.tensor_max(rowB[0:1, i0:i0 + 1], rowB[0:1, i0:i0 + 1], rowC[0:1, 100:101])
                        return e.tensor_max(rowB[0:1, i0 + 1:i0 + 2], rowB[0:1, i0 + 1:i0 + 2],
                                            rowC[0:1, 101:102])
                    nrm_v = P.op("vector", vf_n2, inc="VE")
                    nrm_war = nrm_v
        P.wait("scalar", nrm_v)
        t_sq = P.op("scalar", lambda e: e.activation(rowC[0:1, 0:32], rowB[0:1, 0:32], AF.Sqrt),
                    inc="SE")
        P.wait("vector", t_sq)

        def vf_m(e):
            return e.scalar_tensor_tensor(rowAB[0:1, 0:8], rowC[0:1, 0:8], -0.125,
                                          rowC[0:1, 16:24], ALU.mult, ALU.mult)
        P.waits("vector", [rowab_war[0]])
        t_m = P.op("vector", vf_m, inc="VE")
        P.wait("tensor", t_m)
        t_op1 = P.op("tensor", lambda e: e.matmul(
            ps0[0:8, 0:128], rowAB[0:1, 0:8], ones_row[:], start=True, stop=True), inc="TE")
        rowab_war[0] = t_op1
        P.waits("vector", [t_op1, wout_war[0], wout_war[1]])
        t_op2 = P.op("vector", lambda e: e.tensor_copy(s32a[0:8, 0, 0:128], ps0[0:8, 0:128]),
                     inc="VE")
        P.waits("tensor", [t_op2, t_idf])
        t_op3 = P.op("tensor", lambda e: e.transpose(
            psS2[0:128, 0:8],
            s32a[0:8, 0, 0:128], identf[0:8, 0:8]), inc="TE")
        mcol_s32a_done = t_op3
        P.wait("scalar", t_op3)
        t_mc = P.op("scalar", lambda e: e.activation(
            mcol[:, 0:8], psS2[0:128, 0:8], AF.Identity), inc="SE")

        if ph == 0 and debug_taps and "s2dbg" in debug_taps:
            dq = nc.dram_tensor("dbg_qbuf", [128, 4 * TC], BF16, kind="ExternalOutput").ap()
            dk = nc.dram_tensor("dbg_kbuf", [128, 4 * T], BF16, kind="ExternalOutput").ap()
            dv = nc.dram_tensor("dbg_vbuf", [128, 32 * 8 * 65], BF16, kind="ExternalOutput").ap()
            dm = nc.dram_tensor("dbg_mcol", [128, 8], F32, kind="ExternalOutput").ap()
            P.waits("sync", [t_mc])
            P.dma("sync", dq, qbuf[:].rearrange("p a b -> p (a b)"))
            P.dma("sync", dk, kbuf[:].rearrange("p a b -> p (a b)"))
            P.dma("sync", dv, vbuf[:].rearrange("p a b c -> p (a b c)"))
            tdm = P.dma("sync", dm, mcol[:])
            P.waits("vector", [tdm])
            P.waits("scalar", [tdm])
            P.waits("tensor", [tdm])
            P.waits("gpsimd", [tdm])
            df = nc.dram_tensor("dbg_fenc", [128, T], BF16, kind="ExternalOutput").ap()
            dcs = nc.dram_tensor("dbg_cs", [128, 4 * 512], BF16, kind="ExternalOutput").ap()
            dsn = nc.dram_tensor("dbg_sn", [128, 4 * 512], BF16, kind="ExternalOutput").ap()
            P.dma("sync", df, fenc2[:])
            dwf = nc.dram_tensor("dbg_wf", [32, 64], BF16, kind="ExternalOutput").ap()
            dft = nc.dram_tensor("dbg_ft", [32, 2 * 512], BF16, kind="ExternalOutput").ap()
            P.dma("sync", dwf, wfT_sb[:])
            P.dma("sync", dft, four_t[:].rearrange("p a b -> p (a b)"))
            P.dma("sync", dcs, costile[:].rearrange("p a b -> p (a b)"))
            P.dma("sync", dsn, sintile[:].rearrange("p a b -> p (a b)"))
        pvqb_war = None
        for qb in range(4):
            pv_done = []
            fin_bar = None
            for hp in range(4):
                exp_tok = [None, None]
                pv_tok = [fin_bar, fin_bar]
                for kc in range(32):
                    pp = kc % 2
                    pS = psS if pp == 0 else psS2
                    P.waits("tensor", [t_mc, exp_tok[pp]])

                    def tf_s(e, hp=hp, kc=kc, qb=qb, pS=pS):
                        e.matmul(pS[:, 0:512],
                                 kbuf[0:64, hp, kc * 128:(kc + 1) * 128],
                                 qbuf[0:64, hp, qb * 512:(qb + 1) * 512],
                                 start=True, stop=True, tile_position=(0, 0))
                        return e.matmul(pS[:, 512:1024],
                                        kbuf[64:128, hp, kc * 128:(kc + 1) * 128],
                                        qbuf[64:128, hp, qb * 512:(qb + 1) * 512],
                                        start=True, stop=True, tile_position=(64, 0))
                    ts_s = P.op("tensor", tf_s, inc="TE")
                    P.waits("scalar", [ts_s, pv_tok[pp]])

                    def sf_e(e, hp=hp, pp=pp, pS=pS):
                        e.activation(ptile[:, pp, 0:512], pS[:, 0:512], AF.Exp,
                                     scale=0.125, bias=mcol[:, 2 * hp:2 * hp + 1])
                        return e.activation(ptile[:, pp, 512:1024], pS[:, 512:1024], AF.Exp,
                                            scale=0.125, bias=mcol[:, 2 * hp + 1:2 * hp + 2])
                    te = P.op("scalar", sf_e, inc="SE")
                    exp_tok[pp] = te
                    P.waits("tensor", [te] + ([fin_bar] if kc == 0 else []))

                    def tf_pv(e, hp=hp, kc=kc, pp=pp):
                        e.matmul(ps0[0:65, :], vbuf[:, kc, 2 * hp, :], ptile[:, pp, 0:512],
                                 start=(kc == 0), stop=(kc == 31))
                        return e.matmul(ps1[0:65, :], vbuf[:, kc, 2 * hp + 1, :],
                                        ptile[:, pp, 512:1024],
                                        start=(kc == 0), stop=(kc == 31))
                    tpv = P.op("tensor", tf_pv, inc="TE")
                    pv_tok[pp] = tpv
                P.wait("vector", pv_tok[1])

                P.waits("vector", [rowab_war[0]])
                t_f0 = P.op("vector", lambda e: (
                    e.reciprocal(rowC[0:1, 0:512], ps0[64:65, :]),
                    e.reciprocal(rowC[0:1, 512:1024], ps1[64:65, :]))[-1], inc="VE")
                P.wait("vector", t_f0)
                t_f1 = P.op("vector", lambda e: e.tensor_copy(
                    rowAB[0:1, 0:1024], rowC[0:1, 0:1024]), inc="VE")
                P.wait("tensor", t_f1)

                def tf_rep(e):
                    e.matmul(ps2[0:64, :], ones_row[0:1, 0:64], rowAB[0:1, 0:512],
                             start=True, stop=True)
                    return e.matmul(ps3[0:64, :], ones_row[0:1, 0:64], rowAB[0:1, 512:1024],
                                    start=True, stop=True)
                t_f2 = P.op("tensor", tf_rep, inc="TE")
                rowab_war[0] = t_f2
                P.wait("scalar", t_f2)

                def sf_cp(e, hp=hp):
                    e.activation(s16a[0:64, 0, :], ps0[0:64, :], AF.Copy)
                    e.activation(s16a[0:64, 1, :], ps2[0:64, :], AF.Copy)
                    e.activation(s16b[0:64, 0, :], ps1[0:64, :], AF.Copy)
                    return e.activation(s16b[0:64, 1, :], ps3[0:64, :], AF.Copy)
                t_f3 = P.op("scalar", sf_cp, inc="SE")
                P.waits("vector", [t_f3, pvqb_war])

                def vf_wr(e, hp=hp):
                    e.tensor_mul(pvqb[0:64, hp, :], s16a[0:64, 0, :], s16a[0:64, 1, :])
                    return e.tensor_mul(pvqb[64:128, hp, :], s16b[0:64, 0, :], s16b[0:64, 1, :])
                t_f4 = P.op("vector", vf_wr, inc="VE")
                pv_done.append(t_f4)
                fin_bar = t_f3

            for Dc in range(8):
                par = Dc % 2
                P.waits("gpsimd", [wout_war[par]])
                twp = P.dma("gpsimd", woutp[:, par],
                            woutT[:, Dc * 128:(Dc + 1) * 128].rearrange("(c p) n -> p c n", p=128),
                            sem="WTS")
                P.waits("tensor", pv_done + [twp, wout_war[par]])

                def tf_o(e, ph=ph, par=par):
                    r = None
                    for ic in range(4):
                        r = e.matmul(ps2[:] if par == 0 else ps3[:],
                                     woutp[:, par, ph * 4 + ic, :], pvqb[:, ic, :],
                                     start=(ic == 0), stop=(ic == 3))
                    return r
                t_omm = P.op("tensor", tf_o, inc="TE")
                last_omm = t_omm
                if ph == 0:
                    P.waits("sync", [mcol_s32a_done])
                    tx = P.dma("sync", s32a[:, par, :],
                               xT[Dc * 128:(Dc + 1) * 128, qb * 512:(qb + 1) * 512])
                    P.waits("vector", [t_omm, tx])
                    t_ov = P.op("vector", lambda e, par=par: e.tensor_add(
                        s32a[:, par, :], s32a[:, par, :],
                        (ps2 if par == 0 else ps3)[:]), inc="VE")
                    P.wait("sync", t_ov)
                    t_rx = P.dma("sync", rx_d[Dc * 128:(Dc + 1) * 128, qb * 512:(qb + 1) * 512],
                                 s32a[:, par, :])
                    wout_war[par] = t_rx
                    rx_ph0_done = t_rx
                else:
                    P.waits("vector", [t_omm, wout_war[par], mcol_s32a_done])
                    t_ov = P.op("vector", lambda e, par=par, Dc=Dc: e.tensor_scalar(
                        s32a[:, par, :], (ps2 if par == 0 else ps3)[:],
                        bout_c[:, Dc:Dc + 1], None, ALU.add), inc="VE")
                    P.wait("gpsimd", t_ov)
                    t_rx = P.dma("gpsimd", rx_d[Dc * 128:(Dc + 1) * 128, qb * 512:(qb + 1) * 512],
                                 s32a[:, par, :], accum_op=ALU.add, sem="ACC")
                    wout_war[par] = t_rx
                    rxdma_toks.append(t_rx)
            pvqb_war = last_omm
        att_done_tok = pv_done[-1]
        rope_war = att_done_tok
    s2_done_g = rxdma_toks[-1]
    s2_done_v = att_done_tok
    s2_te_done = last_omm
    if n_stages < 3:
        _finish(P, nc, list(rxdma_toks[-2:]) + [s2_done_v], debug_taps,
                {"xn0": (xn0_d, [D, T], BF16), "rx": (rx_d, [D, TC], F32)})
        with nc.Block() as blk:
            P.run(blk)
        es.close()
        return nc


    w1sb = at("w1sb", [128, NKC, D], BF16, 36)
    w3sb = at("w3sb", [128, NKC, D], BF16, 52)
    w2sb = at("w2sb", [128, NKC, D], BF16, 68)
    rxsl3 = at("rxsl3", [128, 2, NKC, 512], F32, 0)
    xbsl3 = at("xbsl3", [128, 2, NKC, 512], BF16, 84)
    xnrd3 = at("xnrd3", [128, 2, NKC, 512], BF16, 100)
    xssb = at("xssb", [128, NKC, 512], BF16, 116)
    hhb = at("hhb", [128, NKC, 512], BF16, 124)
    gtmp = sb("gtmp", [128, 6], F32)
    msku = sb("msku", [128, 512], mybir.dt.uint8)

    xb_war = [None, None]
    rx3_war = [None, None]
    ln1_st = []
    gate_tok = None
    for tb in range(4):
        par = tb % 2
        P.waits("sync", [s2_done_g, rx3_war[par]])
        t_rx = P.dma("sync", rxsl3[:, par],
                     rx_d[:, tb * 512:(tb + 1) * 512].rearrange("(c p) n -> p c n", p=128))
        P.waits("scalar", [t_rx, xb_war[par], s2_te_done])

        def scast(e, p=par):
            r = None
            for kc in range(NKC):
                r = e.activation(xbsl3[:, p, kc, :], rxsl3[:, p, kc, :], AF.Copy)
            return r
        t_cast = P.op("scalar", scast, inc="SE")
        rx3_war[par] = t_cast
        st_tok = stats_mms(lambda kc, p=par: xbsl3[:, p, kc, :], par, sqs, [t_cast])
        t_ab = ln_rows(512, EPS_LN, st_tok, rms=False, par=par)
        P.waits("vector", [t_ab])

        t_app = P.opseq("vector", [
            lambda e, p=par: [e.tensor_mul(xbsl3[:, p, kc, :], xbsl3[:, p, kc, :],
                                           arep[:, p, 0:512]) for kc in range(NKC)][-1],
            lambda e, p=par: [e.tensor_add(xbsl3[:, p, kc, :], xbsl3[:, p, kc, :],
                                           brep[:, p, 0:512]) for kc in range(NKC)][-1],
        ], inc="VE")
        xb_war[par] = t_app
        P.wait("sync", t_app)
        t_st = P.dma("sync", xn_d[:, tb * 512:(tb + 1) * 512]
                     .rearrange("(c p) n -> p c n", p=128), xbsl3[:, par])
        ln1_st.append(t_st)
        P.waits("tensor", [t_app, t_gw, gate_tok])

        def tf_g(e, p=par):
            r = None
            for kc in range(NKC):
                r = e.matmul(psS2[0:8, 0:512], gwT_sb[:, kc, :], xbsl3[:, p, kc, :],
                             start=(kc == 0), stop=(kc == NKC - 1))
            return r
        t_gmm = P.op("tensor", tf_g, inc="TE")
        xb_war[par] = t_gmm
        P.wait("vector", t_gmm)
        t_l32 = P.op("vector", lambda e: e.tensor_copy(s32a[0:8, 0, 0:512], psS2[0:8, 0:512]),
                     inc="VE")
        gate_sub = None
        for sub in range(4):
            P.waits("tensor", [t_l32, gate_sub, t_idf])
            t_tr = P.op("tensor", lambda e, sub=sub: e.transpose(
                psS[0:128, 0:8], s32a[0:8, 0, sub * 128:(sub + 1) * 128],
                identf[0:8, 0:8]), inc="TE")
            P.wait("vector", t_tr)

            t_v0 = P.op("vector", lambda e: e.tensor_copy(
                s32b[:, 0, 0:8], psS[0:128, 0:8]), inc="VE")
            P.wait("vector", t_v0)
            t_v1 = P.op("vector", lambda e: e.reduce_max(
                gtmp[:, 0:1], s32b[:, 0, 0:8], AX.X, negate=True), inc="VE")
            P.wait("scalar", t_v1)
            t_s1 = P.op("scalar", lambda e: e.activation(
                s32b[:, 1, 0:8], s32b[:, 0, 0:8], AF.Exp, bias=gtmp[:, 0:1],
                accum_out=gtmp[:, 1:2]), inc="SE")
            P.wait("vector", t_s1)

            steps = [
                lambda e: e.reciprocal(gtmp[:, 2:3], gtmp[:, 1:2]),
                lambda e: e.tensor_scalar(s32b[:, 1, 0:8], s32b[:, 1, 0:8], gtmp[:, 2:3], None, ALU.mult),
                lambda e: e.reduce_max(gtmp[:, 3:4], s32b[:, 1, 0:8], AX.X),
                lambda e: e.tensor_scalar(s32b[:, 0, 0:8], s32b[:, 1, 0:8], gtmp[:, 3:4], None, ALU.is_lt),
                lambda e: e.tensor_mul(s32b[:, 0, 0:8], s32b[:, 0, 0:8], s32b[:, 1, 0:8]),
                lambda e: e.reduce_max(gtmp[:, 4:5], s32b[:, 0, 0:8], AX.X),
                lambda e: e.tensor_scalar(s32b[:, 0, 0:8], s32b[:, 1, 0:8], gtmp[:, 4:5], None, ALU.is_ge),
                lambda e, tb=tb, sub=sub: e.tensor_mul(msc[:, tb * 4 + sub, :], s32b[:, 0, 0:8], s32b[:, 1, 0:8]),
            ]
            for fstep in steps:
                gate_sub = P.op("vector", fstep, inc="VE")
                P.wait("vector", gate_sub)
        gate_tok = gate_sub
    msc_done = gate_tok

    expert_list = [("s", None)] + [("r", e) for e in range(8)]
    xs_war = None
    hh_war = None
    xnrd_war = [None, None]
    s32b_war = None
    rx_accum_toks = list(rxdma_toks)

    def load_expert_w3(kind, e, after_h, after_o):
        srcs = (w1s, w3s, w2s) if kind == "s" else (w1[e], w3[e], w2[e])
        P.waits("gpsimd", [after_h, s2_te_done])
        a = P.dma("gpsimd", w1sb[:], srcs[0].rearrange("(c p) n -> p c n", p=128), sem="WTS")
        b = P.dma("gpsimd", w3sb[:], srcs[1].rearrange("(c p) n -> p c n", p=128), sem="WTS")
        P.waits("gpsimd", [after_o])
        c = P.dma("gpsimd", w2sb[:], srcs[2].rearrange("(c p) n -> p c n", p=128), sem="WTS")
        return a, b, c

    t_w1, t_w3, t_w2 = load_expert_w3(*expert_list[0], None, None)
    last_h_mm = None
    last_o_mm = None
    for ei, (kind, e) in enumerate(expert_list):
        cur_w1, cur_w3, cur_w2 = t_w1, t_w3, t_w2
        for tb in range(4):
            par = tb % 2
            P.waits("sync", [ln1_st[tb], xnrd_war[par]])
            t_xn = P.dma("sync", xnrd3[:, par],
                         xn_d[:, tb * 512:(tb + 1) * 512].rearrange("(c p) n -> p c n", p=128))
            if kind == "r":
                tr_tok = None
                for sub in range(4):
                    P.waits("tensor", [msc_done, tr_tok, t_idf])
                    tr_tok = P.op("tensor", lambda e_, e=e, tb=tb, sub=sub: e_.transpose(
                        psS2[0:1, sub * 128:(sub + 1) * 128],
                        msc[:, tb * 4 + sub, e:e + 1], identf[:]), inc="TE")
                P.waits("vector", [tr_tok, rowab_war[0]])
                t_r1 = P.op("vector", lambda e_: e_.tensor_copy(
                    rowAB[0:1, 0:512], psS2[0:1, 0:512]), inc="VE")
                P.wait("tensor", t_r1)
                t_r2r = P.op("tensor", lambda e_: e_.matmul(
                    ps2[:, 0:512], ones_row[:], rowAB[0:1, 0:512],
                    start=True, stop=True), inc="TE")
                rowab_war[0] = t_r2r
                P.wait("vector", t_r2r)
                t_sr = P.op("vector", lambda e_: e_.tensor_copy(
                    s16c[:, 0, :], ps2[:, 0:512]), inc="VE")
                P.waits("vector", [t_xn, xs_war, t_sr])

                def vxs(e_, p=par):
                    r = None
                    for kc in range(NKC):
                        r = e_.tensor_mul(xssb[:, kc, :], xnrd3[:, p, kc, :], s16c[:, 0, :])
                    return r
                t_xs = P.op("vector", vxs, inc="VE")
                rhs_par = None
                rhs_tok = t_xs
            else:
                rhs_par = par
                rhs_tok = t_xn

            def rhs_ap(kc, p=rhs_par, par=par):
                return xssb[:, kc, :] if p is None else xnrd3[:, par, kc, :]

            ev_tok = None
            mul_tok = None
            for Hc in range(NKC):
                P.waits("tensor", [rhs_tok, cur_w1, cur_w3, ev_tok])

                def tf_h(e_, Hc=Hc, f=rhs_ap):
                    for kc in range(NKC):
                        e_.matmul(ps0[:], w1sb[:, kc, Hc * 128:(Hc + 1) * 128], f(kc),
                                  start=(kc == 0), stop=(kc == NKC - 1))
                    r = None
                    for kc in range(NKC):
                        r = e_.matmul(ps1[:], w3sb[:, kc, Hc * 128:(Hc + 1) * 128], f(kc),
                                      start=(kc == 0), stop=(kc == NKC - 1))
                    return r
                t_h = P.op("tensor", tf_h, inc="TE")
                last_h_mm = t_h
                P.waits("scalar", [t_h, mul_tok])

                def sf_h(e_):
                    e_.activation(s16a[:, 0, :], ps0[:], AF.Silu)
                    return e_.activation(s16b[:, 0, :], ps1[:], AF.Copy)
                t_se = P.op("scalar", sf_h, inc="SE")
                ev_tok = t_se
                P.waits("vector", [t_se, hh_war])
                mul_tok = P.op("vector", lambda e_, Hc=Hc: e_.tensor_mul(
                    hhb[:, Hc, :], s16a[:, 0, :], s16b[:, 0, :]), inc="VE")
            hh_ready = mul_tok
            ev_war = [None, None]
            for Dc in range(NKC):
                opar = Dc % 2
                P.waits("tensor", [hh_ready, cur_w2, ev_war[opar]])

                def tf_o3(e_, Dc=Dc, opar=opar):
                    r = None
                    for Hc in range(NKC):
                        r = e_.matmul((ps2 if opar == 0 else ps3)[:],
                                      w2sb[:, Hc, Dc * 128:(Dc + 1) * 128], hhb[:, Hc, :],
                                      start=(Hc == 0), stop=(Hc == NKC - 1))
                    return r
                t_o3 = P.op("tensor", tf_o3, inc="TE")
                last_o_mm = t_o3
                P.waits("vector", [t_o3, s32b_war])
                t_ev = P.op("vector", lambda e_, opar=opar: e_.tensor_copy(
                    s32b[:, opar, :], (ps2 if opar == 0 else ps3)[:]), inc="VE")
                ev_war[opar] = t_ev
                P.wait("gpsimd", t_ev)
                t_acc = P.dma("gpsimd", rx_d[Dc * 128:(Dc + 1) * 128, tb * 512:(tb + 1) * 512],
                              s32b[:, opar, :], accum_op=ALU.add, sem="ACC")
                rx_accum_toks.append(t_acc)
                if opar == 1:
                    s32b_war = t_acc
            xnrd_war[par] = last_h_mm
            xs_war = last_h_mm
            hh_war = last_o_mm
        if ei + 1 < len(expert_list):
            t_w1, t_w3, t_w2 = load_expert_w3(*expert_list[ei + 1], last_h_mm, last_o_mm)
    s3_done_g = rx_accum_toks[-1]
    s3_te = last_o_mm
    if n_stages < 4:
        _finish(P, nc, [s3_done_g, s3_te], debug_taps,
                {"xn0": (xn0_d, [D, T], BF16), "rx": (rx_d, [D, TC], F32),
                 "xn": (xn_d, [D, TC], BF16)})
        with nc.Block() as blk:
            P.run(blk)
        es.close()
        return nc

    ctb = at("ctb", [128, NKC, 512], F32, 36)
    gtb = at("gtb", [128, NKC, 512], F32, 52)
    ybf = at("ybf", [128, NKC, TC], BF16, 68)
    Pbuf = at("Pbuf", [128, NKC, TC], BF16, 100)
    wpph = at("wpph", [128, 2, NKC, 128], BF16, 132)
    wppk = at("wppk", [128, 2, NKC, 128], BF16, 136)
    xn4rd = at("xn4rd", [128, 2, NKC, 512], BF16, 0)
    rxsl4 = at("rxsl4", [128, NKC, 512], F32, 16)
    u4wr = at("u4wr", [128, 2, NKC, 512], BF16, 36)
    agrow = at("agrow", [2, 1024], F32, 152)

    rx4_war = [None, None]
    u4_war = [None, None]
    ln2_st = []
    for tb in range(4):
        par = tb % 2
        P.waits("sync", [s3_done_g, rx4_war[par]])
        t_rx = P.dma("sync", rxsl4[:],
                     rx_d[:, tb * 512:(tb + 1) * 512].rearrange("(c p) n -> p c n", p=128))
        P.waits("scalar", [t_rx, u4_war[par], s3_te, rx4_war[0], rx4_war[1]])

        def scast4(e, p=par):
            r = None
            for kc in range(NKC):
                r = e.activation(u4wr[:, p, kc, :], rxsl4[:, kc, :], AF.Copy)
            return r
        t_cast = P.op("scalar", scast4, inc="SE")
        rx4_war[par] = t_cast
        st_tok = stats_mms(lambda kc, p=par: u4wr[:, p, kc, :], par, sqs, [t_cast])
        t_ab = ln_rows(512, EPS_LN, st_tok, rms=False, par=par)
        P.waits("vector", [t_ab])

        t_app = P.opseq("vector", [
            lambda e, p=par: [e.tensor_mul(u4wr[:, p, kc, :], u4wr[:, p, kc, :],
                                           arep[:, p, 0:512]) for kc in range(NKC)][-1],
            lambda e, p=par: [e.tensor_add(u4wr[:, p, kc, :], u4wr[:, p, kc, :],
                                           brep[:, p, 0:512]) for kc in range(NKC)][-1],
        ], inc="VE")
        u4_war[par] = t_app
        P.wait("sync", t_app)
        t_st = P.dma("sync", xn_d[:, tb * 512:(tb + 1) * 512]
                     .rearrange("(c p) n -> p c n", p=128), u4wr[:, par])
        ln2_st.append(t_st)
    ln2_done = ln2_st[-1]

    wp_war = [None, None]
    xn4_war = [None, None]
    scan_tok = None
    cg_war = s2_done_v
    sg_war = None
    rxld_war = [None, None]
    for tb in range(4):
        par = tb % 2
        P.waits("sync", [ln2_st[tb], xn4_war[par], ln2_done])
        t_xn = P.dma("sync", xn4rd[:, par],
                     xn_d[:, tb * 512:(tb + 1) * 512].rearrange("(c p) n -> p c n", p=128))
        cg_toks = []
        proj_last = None
        for dc in range(NKC):
            wpar = dc % 2
            P.waits("gpsimd", [wp_war[wpar], s3_te])
            t_wh = P.dma("gpsimd", wpph[:, wpar],
                         Wp[:, dc * 128:(dc + 1) * 128].rearrange("(c p) n -> p c n", p=128),
                         sem="WTS")
            t_wk = P.dma("gpsimd", wppk[:, wpar],
                         Wp[:, D + dc * 128:D + (dc + 1) * 128].rearrange("(c p) n -> p c n", p=128),
                         sem="WTS")
            P.waits("tensor", [t_xn, t_wh, t_wk, proj_last, cg_war, sg_war])

            def tf_p4(e, p=par, wpar=wpar):
                for kc in range(NKC):
                    e.matmul(ps0[:], wpph[:, wpar, kc, :], xn4rd[:, p, kc, :],
                             start=(kc == 0), stop=(kc == NKC - 1))
                r = None
                for kc in range(NKC):
                    r = e.matmul(ps1[:], wppk[:, wpar, kc, :], xn4rd[:, p, kc, :],
                                 start=(kc == 0), stop=(kc == NKC - 1))
                return r
            t_pj = P.op("tensor", tf_p4, inc="TE")
            proj_last = t_pj
            wp_war[wpar] = t_pj
            xn4_war[par] = t_pj
            P.waits("scalar", [t_pj, scan_tok])

            def sf_cg(e, dc=dc):
                e.activation(ctb[:, dc, :], ps1[:], AF.Sigmoid, scale=-1.0,
                             bias=bpn_c[:, 8 + dc:8 + dc + 1])
                e.activation(s32a[:, 0, :], ps1[:], AF.Sigmoid,
                             bias=bp_c[:, 8 + dc:8 + dc + 1])
                return e.activation(s32a[:, 1, :], ps0[:], AF.Identity,
                                    bias=bp_c[:, dc:dc + 1])
            t_cg = P.op("scalar", sf_cg, inc="SE")
            P.waits("vector", [t_cg])
            P.waits("scalar", [t_cg])
            t_sg = P.op("scalar", lambda e: e.activation(
                s16a[:, 1, :], s32a[:, 1, :], AF.Sigmoid), inc="SE")
            sg_war = t_cg
            P.waits("vector", [t_sg, cg_war])

            t_g = P.opseq("vector", [
                lambda e: (e.tensor_scalar(s32b[:, 0, :], s32a[:, 1, :], 0.0, None, ALU.is_ge),
                           e.tensor_scalar(s32b[:, 1, :], s32a[:, 1, :], 0.5, None, ALU.add))[-1],
                lambda e: e.tensor_sub(s32b[:, 1, :], s32b[:, 1, :], s16a[:, 1, :]),
                lambda e: e.tensor_mul(s32b[:, 1, :], s32b[:, 1, :], s32b[:, 0, :]),
                lambda e: (e.tensor_add(s16c[:, 1, :], s16a[:, 1, :], s32b[:, 1, :]),
                           e.tensor_copy(s16b[:, 0, :], s32a[:, 0, :]))[-1],
                lambda e, dc=dc: e.tensor_mul(gtb[:, dc, :], s16b[:, 0, :], s16c[:, 1, :]),
            ], inc="VE")
            cg_toks.append(t_g)
            cg_war = t_g
        if tb == 0 and debug_taps and "s4dbg" in debug_taps:
            d_c0 = nc.dram_tensor("dbg_c0", [128, 8 * 512], F32, kind="ExternalOutput").ap()
            d_g0 = nc.dram_tensor("dbg_g0", [128, 8 * 512], F32, kind="ExternalOutput").ap()
            P.waits("sync", cg_toks)
            tc0 = P.dma("sync", d_c0, ctb[:].rearrange("p a b -> p (a b)"))
            tg0 = P.dma("sync", d_g0, gtb[:].rearrange("p a b -> p (a b)"))
            P.waits("vector", [tc0, tg0])
            P.waits("scalar", [tc0, tg0])
        P.waits("vector", cg_toks + [scan_tok])

        for dc in range(NKC):
            tks = P.op("vector", lambda e, dc=dc: e.tensor_tensor_scan(
                s32a[:, 0, :], ctb[:, dc, :], gtb[:, dc, :],
                carry_c[:, dc:dc + 1], ALU.mult, ALU.add), inc="VE")
            P.wait("vector", tks)
            tkc = P.op("vector", lambda e, dc=dc: e.tensor_copy(
                carry_c[:, dc:dc + 1], s32a[:, 0, 511:512]), inc="VE")
            P.wait("vector", tkc)
            tps = P.op("vector", lambda e, dc=dc, tb=tb: e.tensor_tensor_scan(
                Pbuf[:, dc, tb * 512:(tb + 1) * 512], ctb[:, dc, :], zeros_t[:, :],
                pP_c[:, dc:dc + 1], ALU.mult, ALU.add), inc="VE")
            P.wait("vector", tps)
            tpc = P.op("vector", lambda e, dc=dc, tb=tb: e.tensor_copy(
                pP_c[:, dc:dc + 1], Pbuf[:, dc, tb * 512 + 511:tb * 512 + 512]), inc="VE")
            P.wait("vector", tpc)
            scan_tok = P.op("vector", lambda e, dc=dc, tb=tb: e.tensor_copy(
                ybf[:, dc, tb * 512:(tb + 1) * 512], s32a[:, 0, :]), inc="VE")
    scans_done = scan_tok

    P.waits("vector", [scans_done, t_cst])
    t_mask = P.op("vector", lambda e: e.tensor_scalar(
        carry_c[:], carry_c[:], invp_sb[:, 0:1], None, ALU.mult), inc="VE")
    P.wait("gpsimd", t_mask)
    t_agw = P.dma("gpsimd", ag_in, carry_c[:], sem="AGD")
    P.waits("gpsimd", [t_agw])
    t_cc = P.op("gpsimd", lambda e: e.collective_compute(
        "AllGather", ALU.bypass,
        replica_groups=[[0, 1], [2, 3], [4, 5], [6, 7]],
        ins=[ag_in], outs=[ag_out]), inc="CC")
    P.waits("sync", [t_cc])
    t_agr = P.dma("sync", agrow[:], ag_out.rearrange("(r p) ic -> r (p ic)", p=128))
    P.waits("tensor", [t_agr, t_sel])
    t_selmm = P.op("tensor", lambda e: (
        e.matmul(psS2[0:1, 0:512], sel_sb[:], agrow[:, 0:512], start=True, stop=True),
        e.matmul(psS2[0:1, 512:1024], sel_sb[:], agrow[:, 512:1024], start=True, stop=True))[-1],
        inc="TE")
    P.wait("vector", t_selmm)
    t_prow = P.op("vector", lambda e: e.tensor_copy(rowC[0:1, 0:1024], psS2[0:1, 0:1024]),
                  inc="VE")
    P.wait("sync", t_prow)
    t_prd = P.dma("sync", pr_d, rowC[0:1, 0:1024])
    P.waits("sync", [t_prd])
    t_pcar = P.dma("sync", pcar_c[:], pr_d.rearrange("one (p ic) -> p (one ic)", p=128))

    yfix_war = None
    rx_acc4 = []
    s32b_acc_war = [None, None]
    for tb in range(4):
        par = tb % 2
        P.waits("vector", [t_pcar, yfix_war])

        t_fix = None
        for dc in range(NKC):
            t_fix = P.opseq("vector", [
                lambda e, dc=dc, tb=tb: e.tensor_scalar(
                    s16a[:, 0, :], Pbuf[:, dc, tb * 512:(tb + 1) * 512],
                    pcar_c[:, dc:dc + 1], None, ALU.mult),
                lambda e, dc=dc, tb=tb: e.tensor_add(
                    ybf[:, dc, tb * 512:(tb + 1) * 512],
                    ybf[:, dc, tb * 512:(tb + 1) * 512], s16a[:, 0, :]),
            ], inc="VE")
        P.waits("sync", [rxld_war[0], rxld_war[1], t_fix])
        t_rx2 = P.dma("sync", rxsl4[:],
                      rx_d[:, tb * 512:(tb + 1) * 512].rearrange("(c p) n -> p c n", p=128))
        P.waits("vector", [t_rx2])

        def vy(e, tb=tb):
            r = None
            for dc in range(NKC):
                r = e.tensor_add(ybf[:, dc, tb * 512:(tb + 1) * 512],
                                 ybf[:, dc, tb * 512:(tb + 1) * 512],
                                 rxsl4[:, dc, :])
            return r
        t_y = P.op("vector", vy, inc="VE")
        yfix_war = t_y
        st_tok = stats_mms(lambda kc, tb=tb: ybf[:, kc, tb * 512:(tb + 1) * 512], par, sqs,
                           [t_y])
        t_ab = ln_rows(512, EPS_LN1, st_tok, rms=False, par=par)
        P.waits("vector", [t_ab])

        for kc in range(NKC):
            P.waits("vector", [t_ab, s32b_acc_war[kc % 2]])

            t_vk = P.opseq("vector", [
                lambda e, p=par, tb=tb, kc=kc: e.tensor_mul(
                    s16b[:, 0, :], ybf[:, kc, tb * 512:(tb + 1) * 512], arep[:, p, 0:512]),
                lambda e, p=par: e.tensor_add(s16b[:, 0, :], s16b[:, 0, :], brep[:, p, 0:512]),
                lambda e, kc=kc: e.tensor_scalar(s32b[:, kc % 2, :], s16b[:, 0, :],
                                                 gam_c[:, kc:kc + 1], bet_c[:, kc:kc + 1],
                                                 ALU.mult, ALU.add),
            ], inc="VE")
            P.wait("gpsimd", t_vk)
            t_ak = P.dma("gpsimd", rx_d[kc * 128:(kc + 1) * 128, tb * 512:(tb + 1) * 512],
                         s32b[:, kc % 2, :], accum_op=ALU.add, sem="ACC")
            s32b_acc_war[kc % 2] = t_ak
            rx_acc4.append(t_ak)
        rxld_war[par] = t_ak
    s4_done_g = rx_acc4[-1]
    s4_te = t_selmm
    if debug_taps and "s4dbg" in debug_taps:
        d_y = nc.dram_tensor("dbg_y", [128, 8 * TC], BF16, kind="ExternalOutput").ap()
        d_P = nc.dram_tensor("dbg_P", [128, 8 * TC], BF16, kind="ExternalOutput").ap()
        d_car = nc.dram_tensor("dbg_car", [128, 16], F32, kind="ExternalOutput").ap()
        P.waits("sync", [yfix_war, s4_done_g])
        tdy = P.dma("sync", d_y, ybf[:].rearrange("p a b -> p (a b)"))
        P.dma("sync", d_P, Pbuf[:].rearrange("p a b -> p (a b)"))
        P.dma("sync", d_car[:, 0:8], pcar_c[:])
        tdc2 = P.dma("sync", d_car[:, 8:16], carry_c[:])
        P.waits("vector", [tdc2])
        P.waits("scalar", [tdc2])
        P.waits("gpsimd", [tdc2])
        P.waits("tensor", [tdc2])
    if n_stages < 5:
        _finish(P, nc, [s4_done_g, s4_te], debug_taps,
                {"xn0": (xn0_d, [D, T], BF16), "rx": (rx_d, [D, TC], F32),
                 "xn": (xn_d, [D, TC], BF16)})
        with nc.Block() as blk:
            P.run(blk)
        es.close()
        return nc

    w1sb5 = at("w1sb5", [128, NKC, TC], BF16, 36)
    w2sb5 = at("w2sb5", [128, 16, D], BF16, 68)
    hbuf5 = at("hbuf5", [128, 16, 512], BF16, 100)
    ugsl5 = at("ugsl5", [128, 2, NKC, 512], BF16, 116)
    dsb5 = at("dsb5", [128, NKC, 512], F32, 132)
    gr32 = at("gr32", [128, 512], F32, 0)
    fin32 = at("fin32", [128, NKC, 512], F32, 16)

    rx5_war = [None, None]
    u5_war = [None, None]
    u32_war = [None, None]
    ln3_st = []
    dl_st = []
    sc_tok = None
    rms_tok = None
    for tb in range(4):
        par = tb % 2
        P.waits("gpsimd", [s4_done_g, rx5_war[par], s4_te])
        t_rx = P.dma("gpsimd", ugsl5[:, par],
                     rx_d[:, tb * 512:(tb + 1) * 512].rearrange("(c p) n -> p c n", p=128),
                     sem="XS")
        t_cast = t_rx
        st_tok = stats_mms(lambda kc, p=par: ugsl5[:, p, kc, :], par, sqs, [t_cast])
        t_ab = ln_rows(512, EPS_LN, st_tok, rms=False, par=par)
        P.waits("vector", [t_ab])

        t_app = P.opseq("vector", [
            lambda e, p=par: [e.tensor_mul(ugsl5[:, p, kc, :], ugsl5[:, p, kc, :],
                                           arep[:, p, 0:512]) for kc in range(NKC)][-1],
            lambda e, p=par: [e.tensor_add(ugsl5[:, p, kc, :], ugsl5[:, p, kc, :],
                                           brep[:, p, 0:512]) for kc in range(NKC)][-1],
        ], inc="VE")
        u5_war[par] = t_app
        P.wait("sync", t_app)
        t_st = P.dma("sync", xn_d[:, tb * 512:(tb + 1) * 512]
                     .rearrange("(c p) n -> p c n", p=128), ugsl5[:, par])
        ln3_st.append(t_st)
        P.waits("scalar", [t_app, u32_war[par]])

        def s_u32(e, p=par):
            r = None
            for kc in range(NKC):
                r = e.activation(dsb5[:, kc, :], ugsl5[:, p, kc, :], AF.Copy)
            return r
        t_u32 = P.op("scalar", s_u32, inc="SE")
        P.wait("sync", t_u32)
        t_dl = P.dma("sync", dl_d[:, tb * 512:(tb + 1) * 512]
                     .rearrange("(c p) n -> p c n", p=128), dsb5[:])
        dl_st.append(t_dl)
        rx5_war[par] = t_dl
        u32_war[par] = t_dl
        P.waits("tensor", [t_app, t_ce, sc_tok])

        def tf_sc(e, p=par):
            r = None
            for kc in range(NKC):
                r = e.matmul(psS2[0:8, 0:512], cent_sb[:, kc, :], ugsl5[:, p, kc, :],
                             start=(kc == 0), stop=(kc == NKC - 1))
            return r
        t_scm = P.op("tensor", tf_sc, inc="TE")
        P.wait("scalar", t_scm)
        t_sg5 = P.op("scalar", lambda e: e.activation(
            s32a[0:8, 0, 0:512], psS2[0:8, 0:512], AF.Sigmoid), inc="SE")
        sub_tok = None
        for sub in range(4):
            P.waits("tensor", [t_sg5, sub_tok, t_idf])
            t_tr = P.op("tensor", lambda e, sub=sub: e.transpose(
                psS[0:128, 0:8], s32a[0:8, 0, sub * 128:(sub + 1) * 128],
                identf[0:8, 0:8]), inc="TE")
            P.wait("vector", t_tr)

            steps5 = [
                lambda e: e.tensor_copy(s32b[:, 0, 0:8], psS[0:128, 0:8]),
                lambda e: e.reduce_sum(gtmp[:, 1:2], s32b[:, 0, 0:8], AX.X),
                lambda e: e.reciprocal(gtmp[:, 2:3], gtmp[:, 1:2]),
                lambda e: e.reduce_max(gtmp[:, 3:4], s32b[:, 0, 0:8], AX.X),
                lambda e: e.tensor_scalar(s32b[:, 1, 0:8], s32b[:, 0, 0:8], gtmp[:, 3:4], None, ALU.is_lt),
                lambda e: e.tensor_mul(s32b[:, 1, 0:8], s32b[:, 1, 0:8], s32b[:, 0, 0:8]),
                lambda e: e.reduce_max(gtmp[:, 4:5], s32b[:, 1, 0:8], AX.X),
                lambda e: e.tensor_scalar(s32b[:, 1, 0:8], s32b[:, 0, 0:8], gtmp[:, 4:5], None, ALU.is_ge),
                lambda e: e.tensor_mul(s32b[:, 1, 0:8], s32b[:, 1, 0:8], s32b[:, 0, 0:8]),
                lambda e, tb=tb, sub=sub: e.tensor_scalar(gkc[:, tb * 4 + sub, :], s32b[:, 1, 0:8],
                                                          gtmp[:, 2:3], None, ALU.mult),
            ]
            for fstep in steps5:
                sub_tok = P.op("vector", fstep, inc="VE")
                P.wait("vector", sub_tok)
        sc_tok = sub_tok
        st2 = stats_mms(lambda kc, p=par: ugsl5[:, p, kc, :], par, sqs, [t_app], rms=True)
        rms_tok = ln_rows(512, EPS_RMS, st2, rms=True, par=par,
                          save_rows=(rmsrow[0:1, tb * 512:(tb + 1) * 512],))
        rx5_war[par] = rms_tok
    gk_done = sc_tok
    ln3_done = ln3_st[-1]
    rms_done = rms_tok
    dsb_war0 = dl_st[-1]

    ug_st = {}
    u5rd_war = [None, None]
    for grp, (gcol, dst) in enumerate((
            (gsh_c, ug_d), (grt_c, xn_d))):
        toks = []
        for tb in range(4):
            par = tb % 2
            P.waits("sync", [ln3_st[tb], u5rd_war[par]])
            t_u = P.dma("sync", ugsl5[:, par],
                        xn_d[:, tb * 512:(tb + 1) * 512].rearrange("(c p) n -> p c n", p=128))
            src_is_bf = True
            P.waits("vector", [rms_done, t_u, rowab_war[0]])
            t_rb = P.op("vector", lambda e, tb=tb: e.tensor_copy(
                rowAB[0:1, 0:512], rmsrow[0:1, tb * 512:(tb + 1) * 512]), inc="VE")
            P.wait("tensor", t_rb)
            t_rr = P.op("tensor", lambda e: e.matmul(
                ps2[:, 0:512], ones_row[:], rowAB[0:1, 0:512], start=True, stop=True), inc="TE")
            rowab_war[0] = t_rr
            P.wait("vector", t_rr)

            t_ug = P.op("vector", lambda e: e.tensor_copy(s16c[:, 1, :], ps2[:, 0:512]),
                        inc="VE")
            for kc in range(NKC):
                t_ug = P.opseq("vector", [
                    lambda e, p=par, kc=kc: e.tensor_mul(
                        s16b[:, 0, :], ugsl5[:, p, kc, :], s16c[:, 1, :]),
                    lambda e, p=par, kc=kc, gcol=gcol: e.tensor_scalar(
                        ugsl5[:, p, kc, :], s16b[:, 0, :], gcol[:, kc:kc + 1], None, ALU.mult),
                ], inc="VE")
            P.waits("vector", [t_ug])
            P.wait("sync", t_ug)
            t_w = P.dma("sync", dst[:, tb * 512:(tb + 1) * 512]
                        .rearrange("(c p) n -> p c n", p=128), ugsl5[:, par])
            u5rd_war[par] = t_w
            toks.append(t_w)
        ug_st[grp] = toks
    ug_s_done = ug_st[0][-1]
    ug_r_done = ug_st[1][-1]

    vexperts = []
    for e in range(2):
        vexperts.append(("s", e, 0))
        vexperts.append(("s", e, 1))
    for e in range(8):
        vexperts.append(("r", e, 0))
        vexperts.append(("r", e, 1))

    def load_vw(kind, e, hh, after_h, after_o):
        W1x = W1s_i if kind == "s" else W1r_i
        W2x = W2s_i if kind == "s" else W2r_i
        P.waits("gpsimd", [after_h, s4_te])
        a = P.dma("gpsimd", w1sb5[:],
                  W1x[e][:, hh * TC:(hh + 1) * TC].rearrange("(c p) n -> p c n", p=128),
                  sem="WTS")
        P.waits("gpsimd", [after_o])
        b = P.dma("gpsimd", w2sb5[:],
                  W2x[e][hh * TC:(hh + 1) * TC, :].rearrange("(c p) n -> p c n", p=128),
                  sem="WTS")
        return a, b

    tw1, tw2 = load_vw(*vexperts[0], None, None)
    ug5_war = [None, None]
    h5_war = None
    dsb_war = dsb_war0
    last_h5 = None
    last_o5 = None
    dl_acc = []
    for vi, (kind, e, hh) in enumerate(vexperts):
        cw1, cw2 = tw1, tw2
        src_d = ug_d if kind == "s" else xn_d
        src_rdy = ug_s_done if kind == "s" else ug_r_done
        bcol = (b1s_c if kind == "s" else b1r_c)
        b2col = (b2s_c if kind == "s" else b2r_c)
        for tp in range(4):
            par = tp % 2
            P.waits("sync", [src_rdy, ug5_war[par]])
            t_u = P.dma("sync", ugsl5[:, par],
                        src_d[:, tp * 512:(tp + 1) * 512].rearrange("(c p) n -> p c n", p=128))
            if kind == "r":
                sub_tok2 = None
                for sub in range(4):
                    P.waits("tensor", [gk_done, sub_tok2, t_idf])
                    sub_tok2 = P.op("tensor", lambda e_, e=e, tp=tp, sub=sub: e_.transpose(
                        psS2[0:1, sub * 128:(sub + 1) * 128],
                        gkc[:, tp * 4 + sub, e:e + 1], identf[:]), inc="TE")
                P.waits("vector", [sub_tok2, rowab_war[0]])
                t_gr1 = P.op("vector", lambda e_: e_.tensor_copy(
                    rowAB[0:1, 0:512], psS2[0:1, 0:512]), inc="VE")
                P.wait("tensor", t_gr1)
                t_gr2 = P.op("tensor", lambda e_: e_.matmul(
                    ps2[:, 0:512], ones_row[:], rowAB[0:1, 0:512], start=True, stop=True),
                    inc="TE")
                rowab_war[0] = t_gr2
                P.wait("vector", t_gr2)
                t_gr = P.op("vector", lambda e_: e_.tensor_copy(gr32[:, :], ps2[:, 0:512]),
                            inc="VE")
            else:
                t_gr = None
            ev5 = None
            for Hc in range(16):
                P.waits("tensor", [t_u, cw1, ev5])

                def tf_h5(e_, Hc=Hc, p=par):
                    r = None
                    for kc in range(NKC):
                        r = e_.matmul(ps0[:] if Hc % 2 == 0 else ps1[:],
                                      w1sb5[:, kc, Hc * 128:(Hc + 1) * 128],
                                      ugsl5[:, p, kc, :],
                                      start=(kc == 0), stop=(kc == NKC - 1))
                    return r
                t_h5 = P.op("tensor", tf_h5, inc="TE")
                last_h5 = t_h5
                P.waits("scalar", [t_h5, h5_war])
                ev5 = P.op("scalar", lambda e_, Hc=Hc, e=e, hh=hh, bcol=bcol: e_.activation(
                    hbuf5[:, Hc, :], (ps0 if Hc % 2 == 0 else ps1)[:], AF.Gelu,
                    bias=bcol[:, e, hh * 16 + Hc:hh * 16 + Hc + 1]), inc="SE")
            h5_done = ev5
            ep_war = [None, None]
            for Dc in range(NKC):
                opar = Dc % 2
                P.waits("tensor", [h5_done, cw2, ep_war[opar]])

                def tf_o5(e_, Dc=Dc, opar=opar):
                    r = None
                    for Hc in range(16):
                        r = e_.matmul((ps2 if opar == 0 else ps3)[:],
                                      w2sb5[:, Hc, Dc * 128:(Dc + 1) * 128], hbuf5[:, Hc, :],
                                      start=(Hc == 0), stop=(Hc == 15))
                    return r
                t_o5 = P.op("tensor", tf_o5, inc="TE")
                last_o5 = t_o5
                P.waits("vector", [t_o5, dsb_war] + ([t_gr] if kind == "r" else []))
                if kind == "r":
                    def v_ep(e_, Dc=Dc, opar=opar, e=e, hh=hh, b2col=b2col):
                        if hh == 0:
                            return e_.scalar_tensor_tensor(
                                dsb5[:, Dc, :], (ps2 if opar == 0 else ps3)[:],
                                b2col[:, e, Dc:Dc + 1], gr32[:, :], ALU.add, ALU.mult)
                        return e_.scalar_tensor_tensor(
                            dsb5[:, Dc, :], (ps2 if opar == 0 else ps3)[:],
                            0.0, gr32[:, :], ALU.add, ALU.mult)
                else:
                    def v_ep(e_, Dc=Dc, opar=opar, e=e, hh=hh, b2col=b2col):
                        if hh == 0:
                            return e_.tensor_scalar(
                                dsb5[:, Dc, :], (ps2 if opar == 0 else ps3)[:],
                                b2col[:, e, Dc:Dc + 1], None, ALU.add)
                        return e_.tensor_copy(dsb5[:, Dc, :], (ps2 if opar == 0 else ps3)[:])
                t_ep = P.op("vector", v_ep, inc="VE")
                ep_war[opar] = t_ep
            h5_war = last_o5
            P.wait("gpsimd", t_ep)
            t_dacc = P.dma("gpsimd", dl_d[:, tp * 512:(tp + 1) * 512]
                           .rearrange("(c p) n -> p c n", p=128), dsb5[:],
                           accum_op=ALU.add, sem="ACC")
            dl_acc.append(t_dacc)
            dsb_war = t_dacc
            ug5_war[par] = last_h5
        if vi + 1 < len(vexperts):
            tw1, tw2 = load_vw(*vexperts[vi + 1], last_h5, last_o5)
    dl_done = dl_acc[-1]

    fin_war = None
    out_toks = []
    for tb in range(4):
        P.waits("sync", [dl_done, s4_done_g, fin_war])
        t_r = P.dma("sync", fin32[:],
                    rx_d[:, tb * 512:(tb + 1) * 512].rearrange("(c p) n -> p c n", p=128))
        t_d = P.dma("sync", dsb5[:],
                    dl_d[:, tb * 512:(tb + 1) * 512].rearrange("(c p) n -> p c n", p=128))
        P.waits("vector", [t_r, t_d])

        def v_fin(e):
            r = None
            for kc in range(NKC):
                r = e.tensor_add(fin32[:, kc, :], fin32[:, kc, :], dsb5[:, kc, :])
            return r
        t_f = P.op("vector", v_fin, inc="VE")
        P.wait("sync", t_f)
        t_o = P.dma("sync", out[:, tb * 512:(tb + 1) * 512]
                    .rearrange("(c p) n -> p c n", p=128), fin32[:])
        out_toks.append(t_o)
        fin_war = t_o
    _finish(P, nc, out_toks, debug_taps,
            {"xn0": (xn0_d, [D, T], BF16), "rx": (rx_d, [D, TC], F32),
             "dl": (dl_d, [D, TC], F32), "xn": (xn_d, [D, TC], BF16),
             "ug": (ug_d, [D, TC], BF16)})
    with nc.Block() as blk:
        P.run(blk)
    es.close()
    return nc


_CACHE = {}


def _host_tables():
    f32 = np.float32
    freqs = (10000.0 ** (-np.arange(0, DH, 2, dtype=f32) / DH)).astype(f32)
    pos = np.arange(T, dtype=f32)[:, None]
    ang = (pos * freqs).astype(f32)
    s = np.sin(ang).astype(f32)
    c = np.cos(ang).astype(f32)
    s = np.stack([s, s], axis=-1).reshape(T, DH)
    c = np.stack([c, c], axis=-1).reshape(T, DH)
    ff = np.arange(1, NF + 1, dtype=f32)
    fourier = np.concatenate([np.sin(pos * ff), np.cos(pos * ff)], axis=-1).astype(f32)
    cosT = np.ascontiguousarray(np.concatenate([c.T, c.T], axis=0))
    sinT = np.ascontiguousarray(np.concatenate([s.T, s.T], axis=0))
    fourT = np.ascontiguousarray(fourier.T)
    R = np.zeros((DH, DH), np.float32)
    for r in range(32):
        R[r, 2 * r + 1] = -1.0
    for r in range(32, DH):
        R[r, 2 * (r - 32)] = 1.0
    R2 = np.zeros((128, 128), np.float32)
    R2[0:64, 0:64] = R.T
    R2[64:128, 64:128] = R.T
    return cosT, sinT, fourT, R2


def kernel(**inp):
    if "nc" not in _CACHE:
        _CACHE["nc"] = build_program()
    nc = _CACHE["nc"]
    f32 = np.float32
    cosT, sinT, fourT, R2 = _host_tables()

    def C(a):
        return np.ascontiguousarray(a.astype(f32, copy=False))

    shared = dict(
        wqkvT=C(inp["L0_attn_w_qkv"].T),
        wfT=C(inp["L0_attn_w_fproj"].T),
        bfp=C(np.tile(inp["L0_attn_b_fproj"], 2).reshape(1, 128).T),
        woutT=C(inp["L0_attn_w_out"].T),
        bout=C(inp["L0_attn_b_out"].reshape(8, 128).T),
        gwT=C(inp["L0_ffn_gate_w"].T),
        w1=C(inp["L0_ffn_w1"]), w2=C(inp["L0_ffn_w2"]), w3=C(inp["L0_ffn_w3"]),
        w1s=C(inp["L0_ffn_w1s"][0]), w2s=C(inp["L0_ffn_w2s"][0]), w3s=C(inp["L0_ffn_w3s"][0]),
        Wp=C(inp["L1_attn_W"]),
        bp=C(inp["L1_attn_b"].reshape(16, 128).T),
        gam=C(inp["L1_attn_gamma"].reshape(8, 128).T),
        bet=C(inp["L1_attn_beta"].reshape(8, 128).T),
        gsh=C(inp["L1_ffn_g_shared"].reshape(8, 128).T),
        grt=C(inp["L1_ffn_g_routed"].reshape(8, 128).T),
        W1s=C(inp["L1_ffn_W1_s"]), b1s=C(inp["L1_ffn_b1_s"].reshape(2, 32, 128).transpose(2, 0, 1)),
        W2s=C(inp["L1_ffn_W2_s"]), b2s=C(inp["L1_ffn_b2_s"].reshape(2, 8, 128).transpose(2, 0, 1)),
        W1r=C(inp["L1_ffn_W1_r"]), b1r=C(inp["L1_ffn_b1_r"].reshape(8, 32, 128).transpose(2, 0, 1)),
        W2r=C(inp["L1_ffn_W2_r"]), b2r=C(inp["L1_ffn_b2_r"].reshape(8, 8, 128).transpose(2, 0, 1)),
        cent=C(inp["L1_ffn_centroids"]),
        R2=R2,
    )
    x = inp["L0_attn_x"].astype(f32, copy=False)
    in_maps = []
    for c in range(8):
        b, h = c // 2, c % 2
        own = np.arange(h * TC, (h + 1) * TC)
        oth = np.arange((1 - h) * TC, (2 - h) * TC)
        perm = np.concatenate([own, oth])
        m = dict(shared)
        m["xT"] = np.ascontiguousarray(x[b].T[:, perm])
        m["cosT"] = np.ascontiguousarray(cosT[:, perm])
        m["sinT"] = np.ascontiguousarray(sinT[:, perm])
        m["fourT"] = np.ascontiguousarray(fourT[:, perm])
        m["invpar"] = np.full((128, 1), 1.0 - h, dtype=f32)
        s = np.zeros((2, 1), f32)
        s[1 - h, 0] = 1.0
        m["sel2"] = s
        in_maps.append(m)

    res = run_bass_kernel_spmd(nc, in_maps, core_ids=list(range(8)))
    outv = np.empty((4, T, D), np.float32)
    for c in range(8):
        b, h = c // 2, c % 2
        outv[b, h * TC:(h + 1) * TC, :] = res.results[c]["out"].T
    return outv


if __name__ == "__main__":
    build_program()


# version 06
# speedup vs baseline: 1.1117x; 1.1117x over previous
import numpy as np
import ml_dtypes
from contextlib import ExitStack

import concourse.bass as bass
import concourse.mybir as mybir
from concourse.bass_utils import run_bass_kernel_spmd

F32 = mybir.dt.float32
BF16 = mybir.dt.bfloat16
AF = mybir.ActivationFunctionType
ALU = mybir.AluOpType
AX = mybir.AxisListType
BFNP = ml_dtypes.bfloat16

D = 1024
T = 4096
TC = 2048
NKC = 8
DH = 64
NF = 16
EPS_LN = 1e-5
EPS_LN1 = 1e-6
EPS_RMS = float(np.finfo(np.float32).eps)

N_STAGES = 5


def _finish(P, nc, out_toks, debug_taps, tapmap):
    if debug_taps:
        for nm in debug_taps:
            if nm not in tapmap:
                continue
            ap, shp, dt_ = tapmap[nm]
            dbg = nc.dram_tensor("dbg_" + nm, shp, dt_, kind="ExternalOutput").ap()
            P.waits("sync", list(out_toks))
            out_toks.append(P.dma("sync", dbg, ap))
    P.waits("sync", out_toks)


class Prog:
    ENGS = ("sync", "tensor", "vector", "scalar", "gpsimd")

    ROT = ("LD", "WTS", "XS", "ACC", "CS", "ST")

    def __init__(self, nc):
        self.nc = nc
        self.ops = {e: [] for e in self.ENGS}
        self.sems = {}
        self.counts = {}
        for name in ("LD", "WD", "TE", "VE", "SE", "GE", "CC",
                     "CO", "CS", "WTS", "XS", "ACC", "ST", "AGD"):
            self.sems[name] = nc.alloc_semaphore(name)
            self.counts[name] = 0
        for name in self.ROT:
            self.sems[name + "B"] = nc.alloc_semaphore(name + "B")
            self.counts[name + "B"] = 0
        self.rotctr = {name: 0 for name in self.ROT}

    def _inc(self, name, amt):
        self.counts[name] += amt
        return (self.sems[name], self.counts[name])

    def wait(self, eng, token):
        self.waits(eng, [token])

    def waits(self, eng, tokens):
        best = {}
        flat = []
        for t in tokens:
            if isinstance(t, list):
                flat.extend(t)
            else:
                flat.append(t)
        for t in flat:
            if t is None:
                continue
            s, v = t
            k = id(s)
            if k not in best or v > best[k][1]:
                best[k] = (s, v)
        for s, v in best.values():
            if v > 0:
                self.ops[eng].append(lambda e, s=s, v=v: e.wait_ge(s, v))

    def dma(self, queue, out_ap, in_ap, sem=None, **kw):
        base = sem or ("LD" if queue == "sync" else "WD")
        if base in self.ROT:
            self.rotctr[base] ^= 1
            phys = base if self.rotctr[base] == 0 else base + "B"
            other = base + "B" if phys == base else base
        else:
            phys, other = base, None
        sem = self.sems[phys]
        prev = self.counts[phys]
        if prev > 0:
            self.waits(queue, [(sem, prev)])

        def f(e, o=out_ap, i=in_ap, kw=kw, s=sem):
            e.dma_start(out=o, in_=i, **kw).then_inc(s, 16)
        self.ops[queue].append(f)
        tok = self._inc(phys, 16)
        if other is not None and self.counts[other] > 0:
            return [tok, (self.sems[other], self.counts[other])]
        return tok

    def op(self, eng, fn, inc=None):
        if inc:
            sem = self.sems[inc]

            def g(e, fn=fn, sem=sem):
                fn(e).then_inc(sem, 1)
            self.ops[eng].append(g)
            return self._inc(inc, 1)
        self.ops[eng].append(lambda e, fn=fn: fn(e))
        return None

    def opseq(self, eng, fns, inc):
        tok = None
        for i, fn in enumerate(fns):
            if i > 0:
                self.waits(eng, [tok])
            tok = self.op(eng, fn, inc=inc)
        return tok

    def run(self, blk):
        for eng_name in self.ENGS:
            thunks = list(self.ops[eng_name])

            def make(th):
                def body(eng):
                    for f in th:
                        f(eng)
                return body
            getattr(blk, eng_name)(make(thunks))


DEBUG_TAPS = ()


def build_program(n_stages=N_STAGES, debug_taps=None):
    if debug_taps is None:
        debug_taps = DEBUG_TAPS
    nc = bass.Bass("TRN2", target_bir_lowering=False)

    def din(name, shape, dtype=F32):
        return nc.dram_tensor(name, list(shape), dtype, kind="ExternalInput").ap()

    xT = din("xT", [D, T])
    wqkvT = din("wqkvT", [D, 3 * D])
    wfT = din("wfT", [2 * NF, DH])
    bfp = din("bfp", [128, 1])
    woutT = din("woutT", [D, D])
    bout = din("bout", [128, 8])
    gwT = din("gwT", [D, 8])
    w1 = din("w1", [8, D, D])
    w2 = din("w2", [8, D, D])
    w3 = din("w3", [8, D, D])
    w1s = din("w1s", [D, D])
    w2s = din("w2s", [D, D])
    w3s = din("w3s", [D, D])
    Wp = din("Wp", [D, 2 * D])
    bp = din("bp", [128, 16])
    gam = din("gam", [128, 8])
    bet = din("bet", [128, 8])
    gsh = din("gsh", [128, 8])
    grt = din("grt", [128, 8])
    W1s_i = din("W1s", [2, D, 4 * D])
    b1s_i = din("b1s", [128, 2, 32])
    W2s_i = din("W2s", [2, 4 * D, D])
    b2s_i = din("b2s", [128, 2, 8])
    W1r_i = din("W1r", [8, D, 4 * D])
    b1r_i = din("b1r", [128, 8, 32])
    W2r_i = din("W2r", [8, 4 * D, D])
    b2r_i = din("b2r", [128, 8, 8])
    cent = din("cent", [D, 8])
    cosT = din("cosT", [128, T])
    sinT = din("sinT", [128, T])
    fourT = din("fourT", [2 * NF, T])
    R2 = din("R2", [128, 128])
    invpar = din("invpar", [128, 1])
    sel2 = din("sel2", [2, 1])

    out = nc.dram_tensor("out", [D, TC], F32, kind="ExternalOutput").ap()

    xn0_d = nc.dram_tensor("xn0_d", [D, T], BF16, kind="Internal").ap()
    xn_d = nc.dram_tensor("xn_d", [D, TC], BF16, kind="Internal").ap()
    ug_d = nc.dram_tensor("ug_d", [D, TC], BF16, kind="Internal").ap()
    rx_d = nc.dram_tensor("rx_d", [D, TC], F32, kind="Internal").ap()
    dl_d = nc.dram_tensor("dl_d", [D, TC], F32, kind="Internal").ap()
    ag_in = nc.dram_tensor("ag_in", [128, 8], F32, kind="Internal").ap()
    ag_out = nc.dram_tensor("ag_out", [256, 8], F32, kind="Internal").ap()
    pr_d = nc.dram_tensor("pr_d", [1, D], F32, kind="Internal").ap()

    es = ExitStack()

    def sb(name, shape, dtype):
        return es.enter_context(nc.sbuf_tensor(name, list(shape), dtype))

    def psum_t(name, shape, dtype=F32):
        return es.enter_context(nc.psum_tensor(name, list(shape), dtype))

    ones_col = sb("ones_col", [128, 1], BF16)
    ones_row = sb("ones_row", [1, 128], BF16)
    identf = sb("identf", [128, 128], F32)
    ind0 = sb("ind0", [128, 1], BF16)
    ind1 = sb("ind1", [128, 1], BF16)
    zeros_t = sb("zeros_t", [128, 512], F32)
    invp_sb = sb("invp_sb", [128, 1], F32)
    sel_sb = sb("sel_sb", [2, 1], F32)
    bfp_c = sb("bfp_c", [128, 1], F32)
    bout_c = sb("bout_c", [128, 8], F32)
    bp_c = sb("bp_c", [128, 16], F32)
    bpn_c = sb("bpn_c", [128, 16], F32)
    gam_c = sb("gam_c", [128, 8], F32)
    bet_c = sb("bet_c", [128, 8], F32)
    gsh_c = sb("gsh_c", [128, 8], F32)
    grt_c = sb("grt_c", [128, 8], F32)
    b1s_c = sb("b1s_c", [128, 2, 32], F32)
    b2s_c = sb("b2s_c", [128, 2, 8], F32)
    b1r_c = sb("b1r_c", [128, 8, 32], F32)
    b2r_c = sb("b2r_c", [128, 8, 8], F32)
    mcol = sb("mcol", [128, 8], F32)
    carry_c = sb("carry_c", [128, 8], F32)
    pP_c = sb("pP_c", [128, 8], F32)
    pcar_c = sb("pcar_c", [128, 8], F32)
    msc = sb("msc", [128, 16, 8], F32)
    gkc = sb("gkc", [128, 16, 8], F32)
    gwT_sb = sb("gwT_sb", [128, 8, 8], BF16)
    cent_sb = sb("cent_sb", [128, 8, 8], BF16)
    rowA = sb("rowA", [1, 1024], F32)
    rowB = sb("rowB", [1, 1024], F32)
    rowC = sb("rowC", [1, 1024], F32)
    rowAB = sb("rowAB", [1, 1024], BF16)
    rmsrow = sb("rmsrow", [1, 2048], F32)
    s16a = sb("s16a", [128, 2, 512], BF16)
    s16b = sb("s16b", [128, 2, 512], BF16)
    s16c = sb("s16c", [128, 2, 512], BF16)
    s32a = sb("s32a", [128, 2, 512], F32)
    s32b = sb("s32b", [128, 2, 512], F32)
    ptile = sb("ptile", [128, 2, 1024], BF16)
    arep = sb("arep", [128, 2, 512], BF16)
    brep = sb("brep", [128, 2, 512], BF16)
    r2sb = sb("r2sb", [128, 128], BF16)
    wfT_sb = sb("wfT_sb", [32, 64], BF16)

    psS = psum_t("psS", [128, 1024])
    psS2 = psum_t("psS2", [128, 1024])
    ps0 = psum_t("ps0", [128, 512])
    ps1 = psum_t("ps1", [128, 512])
    ps2 = psum_t("ps2", [128, 512])
    ps3 = psum_t("ps3", [128, 512])

    SBUF_TOTAL = 224 * 1024
    used = SBUF_TOTAL - nc.sbuf_bytes_remaining
    ARENA = (used + 63) & ~63
    LIMIT = SBUF_TOTAL - 512

    def at(name, shape, dtype, off_kb):
        off = ARENA + int(off_kb * 1024)
        nbytes = 1
        for s in shape[1:]:
            nbytes *= s
        nbytes *= mybir.dt.size(dtype)
        assert off + nbytes <= LIMIT, f"{name}: arena overflow ({off + nbytes} > {LIMIT})"
        return nc.alloc_sbuf_tensor_at(name, list(shape), dtype, offset=off)

    P = Prog(nc)
    rowab_war = [None]
    lnsq_war = [None]

    t_inv = P.dma("sync", invp_sb[:], invpar)
    t_sel = P.dma("gpsimd", sel_sb[:], sel2, sem="CO")
    t_r2 = P.dma("gpsimd", r2sb[:], R2, sem="CO")
    t_wf = P.dma("gpsimd", wfT_sb[:], wfT, sem="CO")
    t_gw = P.dma("gpsimd", gwT_sb[:], gwT.rearrange("(c p) n -> p c n", p=128), sem="CO")
    t_ce = P.dma("gpsimd", cent_sb[:], cent.rearrange("(c p) n -> p c n", p=128), sem="CO")

    t_idf0 = P.op("gpsimd", lambda e: e.memset(identf[:], 0.0), inc="GE")
    P.waits("gpsimd", [t_idf0])
    t_idf = P.op("gpsimd", lambda e: e.affine_select(
        out=identf[:], in_=identf[:], compare_op=ALU.not_equal, fill=1.0, base=0,
        pattern=[[-1, 128]], channel_multiplier=1), inc="GE")

    def c_init1(e):
        e.memset(ones_col[:], 1.0)
        e.memset(ones_row[:], 1.0)
        e.memset(ind0[:], 0.0)
        e.memset(ind1[:], 0.0)
        e.memset(zeros_t[:], 0.0)
        e.memset(carry_c[:], 0.0)
        return e.memset(pP_c[:], 1.0)

    def c_init2(e):
        e.memset(ind0[0:64, :], 1.0)
        return e.memset(ind1[64:128, :], 1.0)
    t_cst = P.opseq("vector", [c_init1, c_init2], inc="VE")
    P.wait("vector", t_idf)

    t_b = []
    for src_ap, dst in ((bfp, bfp_c[:]), (bout, bout_c[:]), (bp, bp_c[:]),
                        (gam, gam_c[:]), (bet, bet_c[:]), (gsh, gsh_c[:]),
                        (grt, grt_c[:]), (b1s_i, b1s_c[:]), (b2s_i, b2s_c[:]),
                        (b1r_i, b1r_c[:]), (b2r_i, b2r_c[:])):
        t_b.append(P.dma("sync", dst, src_ap))
    wt = t_b[-1]
    t_cols = wt
    P.wait("vector", t_cols)
    t_bpn = P.op("vector", lambda e: e.tensor_scalar(
        bpn_c[:], bp_c[:], -1.0, None, ALU.mult), inc="VE")

    def ln_rows(n_tok, eps, stats_tok, rms, par, save_rows=None):
        P.waits("vector", [stats_tok, lnsq_war[0]])
        if not rms:
            tv = P.opseq("vector", [
                lambda e, n=n_tok: (
                    e.tensor_scalar(rowA[0:1, 0:n], psS[0:1, 0:n], 1.0 / D, None, ALU.mult),
                    e.tensor_scalar(rowB[0:1, 0:n], psS[0:1, 512:512 + n], 1.0 / D, None, ALU.mult))[-1],
                lambda e, n=n_tok: e.tensor_mul(rowC[0:1, 0:n], rowA[0:1, 0:n], rowA[0:1, 0:n]),
                lambda e, n=n_tok: e.tensor_sub(rowB[0:1, 0:n], rowB[0:1, 0:n], rowC[0:1, 0:n]),
                lambda e, n=n_tok: e.tensor_scalar(rowB[0:1, 0:n], rowB[0:1, 0:n], float(eps), None, ALU.add),
            ], inc="VE")
        else:
            tv = P.opseq("vector", [
                lambda e, n=n_tok: e.tensor_scalar(rowB[0:1, 0:n], psS[0:1, 512:512 + n], 1.0 / D, None, ALU.mult),
                lambda e, n=n_tok: e.tensor_scalar(rowB[0:1, 0:n], rowB[0:1, 0:n], float(eps), None, ALU.add),
            ], inc="VE")
        P.wait("scalar", tv)
        ts = P.op("scalar", lambda e, n=n_tok: e.activation(
            rowC[0:1, 0:n], rowB[0:1, 0:n], AF.Sqrt), inc="SE")
        lnsq_war[0] = ts
        P.wait("vector", ts)

        def g1(e, n=n_tok):
            return e.reciprocal(rowB[0:1, 0:n], rowC[0:1, 0:n])

        def g2(e, n=n_tok):
            r = e.tensor_copy(rowAB[0:1, 0:n], rowB[0:1, 0:n])
            if not rms:
                r = e.tensor_mul(rowA[0:1, 0:n], rowA[0:1, 0:n], rowB[0:1, 0:n])
            if save_rows is not None:
                r = e.tensor_copy(save_rows[0], rowB[0:1, 0:n])
            return r

        def g3(e, n=n_tok):
            return e.tensor_scalar(rowA[0:1, 0:n], rowA[0:1, 0:n], -1.0, None, ALU.mult)

        def g4(e, n=n_tok):
            r = e.tensor_copy(rowAB[0:1, 512:512 + n], rowA[0:1, 0:n])
            if save_rows is not None and len(save_rows) > 1:
                r = e.tensor_copy(save_rows[1], rowA[0:1, 0:n])
            return r
        P.waits("vector", [rowab_war[0]])
        groups = [g1, g2] if rms else [g1, g2, g3, g4]
        tv2 = P.opseq("vector", groups, inc="VE")
        P.wait("tensor", tv2)

        def tf(e, n=n_tok):
            r = e.matmul(ps2[:, 0:n], ones_row[:], rowAB[0:1, 0:n], start=True, stop=True)
            if not rms:
                r = e.matmul(ps3[:, 0:n], ones_row[:], rowAB[0:1, 512:512 + n],
                             start=True, stop=True)
            return r
        tt = P.op("tensor", tf, inc="TE")
        rowab_war[0] = tt
        P.wait("vector", tt)

        def vf3(e, n=n_tok, p=par):
            r = e.tensor_copy(arep[:, p, 0:n], ps2[:, 0:n])
            if not rms:
                r = e.tensor_copy(brep[:, p, 0:n], ps3[:, 0:n])
            return r
        return P.op("vector", vf3, inc="VE")

    def stats_mms(slab_ap_fn, par, sq_buf, pre_toks, rms=False):
        st_tok = None
        for kc in range(NKC):
            P.waits("scalar", pre_toks if kc == 0 else [st_tok])
            tsq = P.op("scalar", lambda e, kc=kc: e.activation(
                sq_buf[:, kc % 2, :], slab_ap_fn(kc), AF.Square), inc="SE")
            P.waits("tensor", [tsq, t_cst])

            def tf(e, kc=kc):
                if not rms:
                    e.matmul(psS[0:1, 0:512], ones_col[:], slab_ap_fn(kc),
                             start=(kc == 0), stop=(kc == NKC - 1))
                return e.matmul(psS[0:1, 512:1024], ones_col[:], sq_buf[:, kc % 2, :],
                                start=(kc == 0), stop=(kc == NKC - 1))
            st_tok = P.op("tensor", tf, inc="TE")
        return st_tok

    xslab = at("xslab", [128, 2, NKC, 512], BF16, 0)
    xnslab = at("xnslab", [128, 2, NKC, 512], BF16, 16)
    sqs = at("sqs", [128, 2, 512], BF16, 32)

    slab_war = [None, None]
    xn_war = [None, None]
    ln0_st = []
    for tb in range(8):
        par = tb % 2
        P.waits("gpsimd", [slab_war[par]])
        tk_ld = P.dma("gpsimd", xslab[:, par],
                      xT[:, tb * 512:(tb + 1) * 512].rearrange("(c p) n -> p c n", p=128),
                      sem="XS")
        st_tok = stats_mms(lambda kc, p=par: xslab[:, p, kc, :], par, sqs,
                           [tk_ld, slab_war[par]])
        t_ab = ln_rows(512, EPS_LN, st_tok, rms=False, par=par)
        P.waits("vector", [t_ab, xn_war[par]])

        t_app = P.opseq("vector", [
            lambda e, p=par: [e.tensor_mul(xnslab[:, p, kc, :], xslab[:, p, kc, :],
                                           arep[:, p, 0:512]) for kc in range(NKC)][-1],
            lambda e, p=par: [e.tensor_add(xnslab[:, p, kc, :], xnslab[:, p, kc, :],
                                           brep[:, p, 0:512]) for kc in range(NKC)][-1],
        ], inc="VE")
        slab_war[par] = t_app
        P.wait("sync", t_app)
        t_st = P.dma("sync", xn0_d[:, tb * 512:(tb + 1) * 512]
                     .rearrange("(c p) n -> p c n", p=128), xnslab[:, par])
        xn_war[par] = t_st
        ln0_st.append(t_st)

    s1_done = ln0_st[-1]
    if n_stages < 2:
        _finish(P, nc, list(ln0_st), debug_taps,
                {"xn0": (xn0_d, [D, T], BF16)})
        with nc.Block() as blk:
            P.run(blk)
        es.close()
        return nc

    kbuf = at("kbuf", [128, 4, T], BF16, 36)
    vbuf = at("vbuf", [128, 32, 8, 65], BF16, 68)
    qbuf = at("qbuf", [128, 4, TC], BF16, 102)
    pvqb = at("pvqb", [128, 4, 512], BF16, 26)
    wqkv_sb = at("wqkv_sb", [128, NKC, 1536], BF16, 118)
    costile = at("costile", [128, 4, 512], BF16, 142)
    sintile = at("sintile", [128, 4, 512], BF16, 146)
    woutp = at("woutp", [128, 2, NKC, 128], BF16, 150)
    fenc2 = at("fenc2", [128, T], BF16, 18)
    nslab = at("nslab", [128, 2, NKC, 512], BF16, 0)
    four_t = at("four_t", [32, 2, 512], BF16, 16)

    prev_tok = None
    P.waits("gpsimd", [xn_war[0], xn_war[1], s1_done, slab_war[0], slab_war[1]])
    ft_war = [None, None]
    for tb in range(8):
        par = tb % 2
        P.waits("gpsimd", [ft_war[par]])
        tkf = P.dma("gpsimd", four_t[:, par, :], fourT[:, tb * 512:(tb + 1) * 512], sem="CS")
        P.waits("tensor", [tkf, t_wf, prev_tok, t_cols])
        def tmf_f(e, p=par):
            e.matmul(ps0[0:64, :], wfT_sb[:], four_t[:, p, :], start=True, stop=True)
            return e.matmul(ps0[64:128, :], wfT_sb[:], four_t[:, p, :], start=True, stop=True)
        tmf = P.op("tensor", tmf_f, inc="TE")
        ft_war[par] = tmf
        P.wait("vector", tmf)

        def sff(e, tb=tb):
            return e.tensor_scalar(fenc2[:, tb * 512:(tb + 1) * 512], ps0[:, :],
                                   bfp_c[:, 0:1], None, ALU.add)
        prev_tok = P.op("vector", sff, inc="VE")
    t_fenc = prev_tok
    if debug_taps and "s2dbg" in debug_taps:
        dfe = nc.dram_tensor("dbg_fenc_early", [128, T], BF16, kind="ExternalOutput").ap()
        P.waits("sync", [t_fenc])
        P.dma("sync", dfe, fenc2[:])

    def load_phase_w(ph, after=()):
        toks = list(after)
        res = []
        for j, c0 in enumerate((ph * 512, D + ph * 512, 2 * D + ph * 512)):
            P.waits("gpsimd", toks)
            res.append(P.dma("gpsimd", wqkv_sb[:, :, j * 512:(j + 1) * 512],
                             wqkvT[:, c0:c0 + 512].rearrange("(c p) n -> p c n", p=128),
                             sem="WTS"))
        return res[-1]

    t_wph = load_phase_w(0)

    rope_war = None
    att_done_tok = None
    wout_war = [None, None]
    rxdma_toks = []

    rx_ph0_done = None
    dbg_kpre = None
    dbg_kpre_tok = [None]
    if debug_taps and "s2kpre" in debug_taps:
        dbg_kpre = nc.dram_tensor("dbg_kpre", [128, T], BF16, kind="ExternalOutput").ap()
        dbg_rot = nc.dram_tensor("dbg_rot", [128, T], F32, kind="ExternalOutput").ap()
        dbg_st = nc.dram_tensor("dbg_st", [128, 4, 512], BF16, kind="ExternalOutput").ap()
    for ph in range(2):
        wtok = t_wph
        if ph == 1:
            P.waits("gpsimd", [rx_ph0_done])
        nslab_war = [None, None]
        cs_war = [None, None, None, None]
        last_rope = None
        vev_tok = None
        for tb in range(8):
            par = tb % 2
            par4 = tb % 4
            P.waits("gpsimd", [nslab_war[par], s1_done])
            tk_ns = P.dma("gpsimd", nslab[:, par],
                          xn0_d[:, tb * 512:(tb + 1) * 512].rearrange("(c p) n -> p c n", p=128),
                          sem="XS")
            P.waits("gpsimd", [cs_war[par4]])
            tk_cs = P.dma("gpsimd", costile[:, par4, :], cosT[:, tb * 512:(tb + 1) * 512], sem="CS")
            tk_sn = P.dma("gpsimd", sintile[:, par4, :], sinT[:, tb * 512:(tb + 1) * 512], sem="CS")
            is_q = tb < 4

            def emit_rope(oc, wcol0, dstbuf, dst_tb):
                nonlocal last_rope

                def tf_p(e, p=par, oc=oc, w0=wcol0):
                    r = None
                    for kc in range(NKC):
                        r = e.matmul((ps0 if oc % 2 == 0 else ps1)[:],
                                     wqkv_sb[:, kc, w0 + oc * 128:w0 + (oc + 1) * 128],
                                     nslab[:, p, kc, :],
                                     start=(kc == 0), stop=(kc == NKC - 1))
                    return r
                P.waits("tensor", [tk_ns, wtok, att_done_tok, last_rope, t_fenc])
                t_mm = P.op("tensor", tf_p, inc="TE")
                P.waits("scalar", [t_mm, last_rope])
                t_pre = P.op("scalar", lambda e, p=par, oc=oc: e.activation(
                    s16a[:, p, :], (ps0 if oc % 2 == 0 else ps1)[:], AF.Copy), inc="SE")
                if dbg_kpre is not None and ph == 0 and dstbuf is kbuf and oc == 0:
                    P.waits("sync", [t_pre])
                    dbg_kpre_tok[0] = P.dma("sync", dbg_kpre[:, tb * 512:(tb + 1) * 512],
                                            s16a[:, par, :])
                    P.waits("scalar", [dbg_kpre_tok[0]])
                    P.waits("tensor", [dbg_kpre_tok[0]])
                P.waits("tensor", [t_pre, t_r2])
                t_rot = P.op("tensor", lambda e, p=par: e.matmul(
                    psS2[:, 0:512], r2sb[:], s16a[:, p, :], start=True, stop=True), inc="TE")
                if dbg_kpre is not None and ph == 0 and dstbuf is kbuf and oc == 0:
                    P.waits("vector", [t_rot])
                    tvx = P.op("vector", lambda e: e.tensor_copy(s32a[:, 0, :], psS2[:, 0:512]), inc="VE")
                    P.waits("sync", [tvx])
                    tdx = P.dma("sync", dbg_rot[:, tb * 512:(tb + 1) * 512], s32a[:, 0, :])
                    P.waits("vector", [tdx])
                    P.waits("tensor", [tdx])
                P.waits("vector", [t_rot, tk_cs, tk_sn, t_fenc])

                last_rope = P.opseq("vector", [
                    lambda e, p=par, p4=par4: (
                        e.tensor_mul(s16b[:, p, :], s16a[:, p, :], costile[:, p4, :]),
                        e.tensor_copy(s16c[:, p, :], psS2[:, 0:512]))[-1],
                    lambda e, p=par, p4=par4: e.tensor_mul(s16c[:, p, :], s16c[:, p, :],
                                                           sintile[:, p4, :]),
                    lambda e, p=par, oc=oc, dst=dstbuf, dtb=dst_tb: e.tensor_add(
                        dst[:, oc, dtb * 512:(dtb + 1) * 512], s16b[:, p, :], s16c[:, p, :]),
                    lambda e, p=par, oc=oc, dst=dstbuf, dtb=dst_tb, tb=tb: e.tensor_add(
                        dst[:, oc, dtb * 512:(dtb + 1) * 512],
                        dst[:, oc, dtb * 512:(dtb + 1) * 512],
                        fenc2[:, tb * 512:(tb + 1) * 512]),
                ], inc="VE")
                P.waits("tensor", [last_rope])
                P.waits("scalar", [last_rope])
                P.waits("gpsimd", [last_rope])
                return last_rope

            for oc in range(4):
                emit_rope(oc, 512, kbuf, tb)
                if is_q:
                    emit_rope(oc, 0, qbuf, tb)
            for tt in range(4):
                gt = tb * 4 + tt
                P.waits("tensor", [tk_ns, wtok, att_done_tok, vev_tok])

                def tf_v(e, p=par, tt=tt):
                    r = None
                    for kc in range(NKC):
                        r = e.matmul(ps2[:],
                                     nslab[:, p, kc, tt * 128:(tt + 1) * 128],
                                     wqkv_sb[:, kc, 1024:1536],
                                     start=(kc == 0), stop=(kc == NKC - 1))
                    return r
                tv_mm = P.op("tensor", tf_v, inc="TE")
                P.wait("vector", tv_mm)

                def vf_v(e, gt=gt):
                    e.tensor_copy(vbuf[:, gt, :, 0:64],
                                  ps2[:].rearrange("p (h d) -> p h d", d=64))
                    return e.memset(vbuf[:, gt, :, 64:65], 1.0)
                vev_tok = P.op("vector", vf_v, inc="VE")
            nslab_war[par] = vev_tok if not is_q else last_rope
            nslab_war[par] = last_rope if last_rope else vev_tok
            if vev_tok is not None and last_rope is not None:
                nslab_war[par] = max([last_rope, vev_tok], key=lambda t: t[1])
            cs_war[par4] = last_rope

        t_qkv = nslab_war[0] if nslab_war[0][1] > nslab_war[1][1] else nslab_war[1]

        if ph == 0:
            P.waits("gpsimd", [t_qkv])
            t_wph = load_phase_w(1)

        nrm_v = t_qkv
        nrm_war = None
        for (buf, ntb, base_idx) in ((qbuf, 4, 0), (kbuf, 8, 16)):
            for oc in range(4):
                for tb in range(ntb):
                    P.waits("vector", [nrm_v, rope_war])
                    tsq = P.op("vector", lambda e, b=buf, oc=oc, tb=tb: e.tensor_mul(
                        s16a[:, tb % 2, :], b[:, oc, tb * 512:(tb + 1) * 512],
                        b[:, oc, tb * 512:(tb + 1) * 512]), inc="VE")
                    P.waits("tensor", [tsq, nrm_war])

                    def tf_n(e, tb=tb):
                        e.matmul(ps0[0:1, :], ind0[:], s16a[:, tb % 2, :], start=True, stop=True)
                        return e.matmul(ps1[0:1, :], ind1[:], s16a[:, tb % 2, :], start=True, stop=True)
                    tn = P.op("tensor", tf_n, inc="TE")
                    P.wait("vector", tn)

                    def vf_n1(e):
                        e.reduce_max(rowC[0:1, 100:101], ps0[0:1, :], AX.X)
                        return e.reduce_max(rowC[0:1, 101:102], ps1[0:1, :], AX.X)
                    t_n1 = P.op("vector", vf_n1, inc="VE")
                    P.wait("vector", t_n1)

                    def vf_n2(e, oc=oc, tb=tb, base=base_idx):
                        i0 = base + 2 * oc
                        if tb == 0:
                            e.tensor_copy(rowB[0:1, i0:i0 + 1], rowC[0:1, 100:101])
                            return e.tensor_copy(rowB[0:1, i0 + 1:i0 + 2], rowC[0:1, 101:102])
                        e.tensor_max(rowB[0:1, i0:i0 + 1], rowB[0:1, i0:i0 + 1], rowC[0:1, 100:101])
                        return e.tensor_max(rowB[0:1, i0 + 1:i0 + 2], rowB[0:1, i0 + 1:i0 + 2],
                                            rowC[0:1, 101:102])
                    nrm_v = P.op("vector", vf_n2, inc="VE")
                    nrm_war = nrm_v
        P.wait("scalar", nrm_v)
        t_sq = P.op("scalar", lambda e: e.activation(rowC[0:1, 0:32], rowB[0:1, 0:32], AF.Sqrt),
                    inc="SE")
        P.wait("vector", t_sq)

        def vf_m(e):
            return e.scalar_tensor_tensor(rowAB[0:1, 0:8], rowC[0:1, 0:8], -0.125,
                                          rowC[0:1, 16:24], ALU.mult, ALU.mult)
        P.waits("vector", [rowab_war[0]])
        t_m = P.op("vector", vf_m, inc="VE")
        P.wait("tensor", t_m)
        t_op1 = P.op("tensor", lambda e: e.matmul(
            ps0[0:8, 0:128], rowAB[0:1, 0:8], ones_row[:], start=True, stop=True), inc="TE")
        rowab_war[0] = t_op1
        P.waits("vector", [t_op1, wout_war[0], wout_war[1]])
        t_op2 = P.op("vector", lambda e: e.tensor_copy(s32a[0:8, 0, 0:128], ps0[0:8, 0:128]),
                     inc="VE")
        P.waits("tensor", [t_op2, t_idf])
        t_op3 = P.op("tensor", lambda e: e.transpose(
            psS2[0:128, 0:8],
            s32a[0:8, 0, 0:128], identf[0:8, 0:8]), inc="TE")
        mcol_s32a_done = t_op3
        P.wait("scalar", t_op3)
        t_mc = P.op("scalar", lambda e: e.activation(
            mcol[:, 0:8], psS2[0:128, 0:8], AF.Identity), inc="SE")

        if ph == 0 and debug_taps and "s2dbg" in debug_taps:
            dq = nc.dram_tensor("dbg_qbuf", [128, 4 * TC], BF16, kind="ExternalOutput").ap()
            dk = nc.dram_tensor("dbg_kbuf", [128, 4 * T], BF16, kind="ExternalOutput").ap()
            dv = nc.dram_tensor("dbg_vbuf", [128, 32 * 8 * 65], BF16, kind="ExternalOutput").ap()
            dm = nc.dram_tensor("dbg_mcol", [128, 8], F32, kind="ExternalOutput").ap()
            P.waits("sync", [t_mc])
            P.dma("sync", dq, qbuf[:].rearrange("p a b -> p (a b)"))
            P.dma("sync", dk, kbuf[:].rearrange("p a b -> p (a b)"))
            P.dma("sync", dv, vbuf[:].rearrange("p a b c -> p (a b c)"))
            tdm = P.dma("sync", dm, mcol[:])
            P.waits("vector", [tdm])
            P.waits("scalar", [tdm])
            P.waits("tensor", [tdm])
            P.waits("gpsimd", [tdm])
            df = nc.dram_tensor("dbg_fenc", [128, T], BF16, kind="ExternalOutput").ap()
            dcs = nc.dram_tensor("dbg_cs", [128, 4 * 512], BF16, kind="ExternalOutput").ap()
            dsn = nc.dram_tensor("dbg_sn", [128, 4 * 512], BF16, kind="ExternalOutput").ap()
            P.dma("sync", df, fenc2[:])
            dwf = nc.dram_tensor("dbg_wf", [32, 64], BF16, kind="ExternalOutput").ap()
            dft = nc.dram_tensor("dbg_ft", [32, 2 * 512], BF16, kind="ExternalOutput").ap()
            P.dma("sync", dwf, wfT_sb[:])
            P.dma("sync", dft, four_t[:].rearrange("p a b -> p (a b)"))
            P.dma("sync", dcs, costile[:].rearrange("p a b -> p (a b)"))
            P.dma("sync", dsn, sintile[:].rearrange("p a b -> p (a b)"))
        pvqb_war = None
        for qb in range(4):
            pv_done = []
            fin_bar = None
            for hp in range(4):
                exp_tok = [None, None]
                pv_tok = [fin_bar, fin_bar]
                for kc in range(32):
                    pp = kc % 2
                    pS = psS if pp == 0 else psS2
                    P.waits("tensor", [t_mc, exp_tok[pp]])

                    def tf_s(e, hp=hp, kc=kc, qb=qb, pS=pS):
                        e.matmul(pS[:, 0:512],
                                 kbuf[0:64, hp, kc * 128:(kc + 1) * 128],
                                 qbuf[0:64, hp, qb * 512:(qb + 1) * 512],
                                 start=True, stop=True, tile_position=(0, 0))
                        return e.matmul(pS[:, 512:1024],
                                        kbuf[64:128, hp, kc * 128:(kc + 1) * 128],
                                        qbuf[64:128, hp, qb * 512:(qb + 1) * 512],
                                        start=True, stop=True, tile_position=(64, 0))
                    ts_s = P.op("tensor", tf_s, inc="TE")
                    P.waits("scalar", [ts_s, pv_tok[pp]])

                    def sf_e(e, hp=hp, pp=pp, pS=pS):
                        e.activation(ptile[:, pp, 0:512], pS[:, 0:512], AF.Exp,
                                     scale=0.125, bias=mcol[:, 2 * hp:2 * hp + 1])
                        return e.activation(ptile[:, pp, 512:1024], pS[:, 512:1024], AF.Exp,
                                            scale=0.125, bias=mcol[:, 2 * hp + 1:2 * hp + 2])
                    te = P.op("scalar", sf_e, inc="SE")
                    exp_tok[pp] = te
                    P.waits("tensor", [te] + ([fin_bar] if kc == 0 else []))

                    def tf_pv(e, hp=hp, kc=kc, pp=pp):
                        e.matmul(ps0[0:65, :], vbuf[:, kc, 2 * hp, :], ptile[:, pp, 0:512],
                                 start=(kc == 0), stop=(kc == 31))
                        return e.matmul(ps1[0:65, :], vbuf[:, kc, 2 * hp + 1, :],
                                        ptile[:, pp, 512:1024],
                                        start=(kc == 0), stop=(kc == 31))
                    tpv = P.op("tensor", tf_pv, inc="TE")
                    pv_tok[pp] = tpv
                P.wait("vector", pv_tok[1])

                P.waits("vector", [rowab_war[0]])
                t_f0 = P.op("vector", lambda e: (
                    e.reciprocal(rowC[0:1, 0:512], ps0[64:65, :]),
                    e.reciprocal(rowC[0:1, 512:1024], ps1[64:65, :]))[-1], inc="VE")
                P.wait("vector", t_f0)
                t_f1 = P.op("vector", lambda e: e.tensor_copy(
                    rowAB[0:1, 0:1024], rowC[0:1, 0:1024]), inc="VE")
                P.wait("tensor", t_f1)

                def tf_rep(e):
                    e.matmul(ps2[0:64, :], ones_row[0:1, 0:64], rowAB[0:1, 0:512],
                             start=True, stop=True)
                    return e.matmul(ps3[0:64, :], ones_row[0:1, 0:64], rowAB[0:1, 512:1024],
                                    start=True, stop=True)
                t_f2 = P.op("tensor", tf_rep, inc="TE")
                rowab_war[0] = t_f2
                P.wait("scalar", t_f2)

                def sf_cp(e, hp=hp):
                    e.activation(s16a[0:64, 0, :], ps0[0:64, :], AF.Copy)
                    e.activation(s16a[0:64, 1, :], ps2[0:64, :], AF.Copy)
                    e.activation(s16b[0:64, 0, :], ps1[0:64, :], AF.Copy)
                    return e.activation(s16b[0:64, 1, :], ps3[0:64, :], AF.Copy)
                t_f3 = P.op("scalar", sf_cp, inc="SE")
                P.waits("vector", [t_f3, pvqb_war])

                def vf_wr(e, hp=hp):
                    e.tensor_mul(pvqb[0:64, hp, :], s16a[0:64, 0, :], s16a[0:64, 1, :])
                    return e.tensor_mul(pvqb[64:128, hp, :], s16b[0:64, 0, :], s16b[0:64, 1, :])
                t_f4 = P.op("vector", vf_wr, inc="VE")
                pv_done.append(t_f4)
                fin_bar = t_f3

            for Dc in range(8):
                par = Dc % 2
                P.waits("gpsimd", [wout_war[par]])
                twp = P.dma("gpsimd", woutp[:, par],
                            woutT[:, Dc * 128:(Dc + 1) * 128].rearrange("(c p) n -> p c n", p=128),
                            sem="WTS")
                P.waits("tensor", pv_done + [twp, wout_war[par]])

                def tf_o(e, ph=ph, par=par):
                    r = None
                    for ic in range(4):
                        r = e.matmul(ps2[:] if par == 0 else ps3[:],
                                     woutp[:, par, ph * 4 + ic, :], pvqb[:, ic, :],
                                     start=(ic == 0), stop=(ic == 3))
                    return r
                t_omm = P.op("tensor", tf_o, inc="TE")
                last_omm = t_omm
                if ph == 0:
                    P.waits("sync", [mcol_s32a_done])
                    tx = P.dma("sync", s32a[:, par, :],
                               xT[Dc * 128:(Dc + 1) * 128, qb * 512:(qb + 1) * 512])
                    P.waits("vector", [t_omm, tx])
                    t_ov = P.op("vector", lambda e, par=par: e.tensor_add(
                        s32a[:, par, :], s32a[:, par, :],
                        (ps2 if par == 0 else ps3)[:]), inc="VE")
                    P.wait("sync", t_ov)
                    t_rx = P.dma("sync", rx_d[Dc * 128:(Dc + 1) * 128, qb * 512:(qb + 1) * 512],
                                 s32a[:, par, :])
                    wout_war[par] = t_rx
                    rx_ph0_done = t_rx
                else:
                    P.waits("vector", [t_omm, wout_war[par], mcol_s32a_done])
                    t_ov = P.op("vector", lambda e, par=par, Dc=Dc: e.tensor_scalar(
                        s32a[:, par, :], (ps2 if par == 0 else ps3)[:],
                        bout_c[:, Dc:Dc + 1], None, ALU.add), inc="VE")
                    P.wait("gpsimd", t_ov)
                    t_rx = P.dma("gpsimd", rx_d[Dc * 128:(Dc + 1) * 128, qb * 512:(qb + 1) * 512],
                                 s32a[:, par, :], accum_op=ALU.add, sem="ACC")
                    wout_war[par] = t_rx
                    rxdma_toks.append(t_rx)
            pvqb_war = last_omm
        att_done_tok = pv_done[-1]
        rope_war = att_done_tok
    s2_done_g = rxdma_toks[-1]
    s2_done_v = att_done_tok
    s2_te_done = last_omm
    if n_stages < 3:
        _finish(P, nc, list(rxdma_toks[-2:]) + [s2_done_v], debug_taps,
                {"xn0": (xn0_d, [D, T], BF16), "rx": (rx_d, [D, TC], F32)})
        with nc.Block() as blk:
            P.run(blk)
        es.close()
        return nc


    w1sb = at("w1sb", [128, NKC, D], BF16, 36)
    w3sb = at("w3sb", [128, NKC, D], BF16, 52)
    w2sb = at("w2sb", [128, NKC, D], BF16, 68)
    rxsl3 = at("rxsl3", [128, 2, NKC, 512], F32, 0)
    xbsl3 = at("xbsl3", [128, 2, NKC, 512], BF16, 84)
    xnrd3 = at("xnrd3", [128, 2, NKC, 512], BF16, 100)
    xssb = at("xssb", [128, NKC, 512], BF16, 116)
    hhb = at("hhb", [128, NKC, 512], BF16, 124)
    gtmp = sb("gtmp", [128, 6], F32)
    msku = sb("msku", [128, 512], mybir.dt.uint8)

    xb_war = [None, None]
    rx3_war = [None, None]
    ln1_st = []
    gate_tok = None
    for tb in range(4):
        par = tb % 2
        P.waits("sync", [s2_done_g, rx3_war[par]])
        t_rx = P.dma("sync", rxsl3[:, par],
                     rx_d[:, tb * 512:(tb + 1) * 512].rearrange("(c p) n -> p c n", p=128))
        P.waits("scalar", [t_rx, xb_war[par], s2_te_done])

        def scast(e, p=par):
            r = None
            for kc in range(NKC):
                r = e.activation(xbsl3[:, p, kc, :], rxsl3[:, p, kc, :], AF.Copy)
            return r
        t_cast = P.op("scalar", scast, inc="SE")
        rx3_war[par] = t_cast
        st_tok = stats_mms(lambda kc, p=par: xbsl3[:, p, kc, :], par, sqs, [t_cast])
        t_ab = ln_rows(512, EPS_LN, st_tok, rms=False, par=par)
        P.waits("vector", [t_ab])

        t_app = P.opseq("vector", [
            lambda e, p=par: [e.tensor_mul(xbsl3[:, p, kc, :], xbsl3[:, p, kc, :],
                                           arep[:, p, 0:512]) for kc in range(NKC)][-1],
            lambda e, p=par: [e.tensor_add(xbsl3[:, p, kc, :], xbsl3[:, p, kc, :],
                                           brep[:, p, 0:512]) for kc in range(NKC)][-1],
        ], inc="VE")
        xb_war[par] = t_app
        P.wait("sync", t_app)
        t_st = P.dma("sync", xn_d[:, tb * 512:(tb + 1) * 512]
                     .rearrange("(c p) n -> p c n", p=128), xbsl3[:, par])
        ln1_st.append(t_st)
        P.waits("tensor", [t_app, t_gw, gate_tok])

        def tf_g(e, p=par):
            r = None
            for kc in range(NKC):
                r = e.matmul(psS2[0:8, 0:512], gwT_sb[:, kc, :], xbsl3[:, p, kc, :],
                             start=(kc == 0), stop=(kc == NKC - 1))
            return r
        t_gmm = P.op("tensor", tf_g, inc="TE")
        xb_war[par] = t_gmm
        P.wait("vector", t_gmm)
        t_l32 = P.op("vector", lambda e: e.tensor_copy(s32a[0:8, 0, 0:512], psS2[0:8, 0:512]),
                     inc="VE")
        gate_sub = None
        for sub in range(4):
            P.waits("tensor", [t_l32, gate_sub, t_idf])
            t_tr = P.op("tensor", lambda e, sub=sub: e.transpose(
                psS[0:128, 0:8], s32a[0:8, 0, sub * 128:(sub + 1) * 128],
                identf[0:8, 0:8]), inc="TE")
            P.wait("vector", t_tr)

            t_v0 = P.op("vector", lambda e: e.tensor_copy(
                s32b[:, 0, 0:8], psS[0:128, 0:8]), inc="VE")
            P.wait("vector", t_v0)
            t_v1 = P.op("vector", lambda e: e.reduce_max(
                gtmp[:, 0:1], s32b[:, 0, 0:8], AX.X, negate=True), inc="VE")
            P.wait("scalar", t_v1)
            t_s1 = P.op("scalar", lambda e: e.activation(
                s32b[:, 1, 0:8], s32b[:, 0, 0:8], AF.Exp, bias=gtmp[:, 0:1],
                accum_out=gtmp[:, 1:2]), inc="SE")
            P.wait("vector", t_s1)

            steps = [
                lambda e: e.reciprocal(gtmp[:, 2:3], gtmp[:, 1:2]),
                lambda e: e.tensor_scalar(s32b[:, 1, 0:8], s32b[:, 1, 0:8], gtmp[:, 2:3], None, ALU.mult),
                lambda e: e.reduce_max(gtmp[:, 3:4], s32b[:, 1, 0:8], AX.X),
                lambda e: e.tensor_scalar(s32b[:, 0, 0:8], s32b[:, 1, 0:8], gtmp[:, 3:4], None, ALU.is_lt),
                lambda e: e.tensor_mul(s32b[:, 0, 0:8], s32b[:, 0, 0:8], s32b[:, 1, 0:8]),
                lambda e: e.reduce_max(gtmp[:, 4:5], s32b[:, 0, 0:8], AX.X),
                lambda e: e.tensor_scalar(s32b[:, 0, 0:8], s32b[:, 1, 0:8], gtmp[:, 4:5], None, ALU.is_ge),
                lambda e, tb=tb, sub=sub: e.tensor_mul(msc[:, tb * 4 + sub, :], s32b[:, 0, 0:8], s32b[:, 1, 0:8]),
            ]
            for fstep in steps:
                gate_sub = P.op("vector", fstep, inc="VE")
                P.wait("vector", gate_sub)
        gate_tok = gate_sub
    msc_done = gate_tok

    expert_list = [("s", None)] + [("r", e) for e in range(8)]
    xs_war = None
    hh_war = None
    xnrd_war = [None, None]
    s32b_war = None
    rx_accum_toks = list(rxdma_toks)

    def load_expert_w3(kind, e, after_h, after_o):
        srcs = (w1s, w3s, w2s) if kind == "s" else (w1[e], w3[e], w2[e])
        P.waits("gpsimd", [after_h, s2_te_done])
        a = P.dma("gpsimd", w1sb[:], srcs[0].rearrange("(c p) n -> p c n", p=128), sem="WTS")
        b = P.dma("gpsimd", w3sb[:], srcs[1].rearrange("(c p) n -> p c n", p=128), sem="WTS")
        P.waits("gpsimd", [after_o])
        c = P.dma("gpsimd", w2sb[:], srcs[2].rearrange("(c p) n -> p c n", p=128), sem="WTS")
        return a, b, c

    t_w1, t_w3, t_w2 = load_expert_w3(*expert_list[0], None, None)
    last_h_mm = None
    last_o_mm = None
    for ei, (kind, e) in enumerate(expert_list):
        cur_w1, cur_w3, cur_w2 = t_w1, t_w3, t_w2
        for tb in range(4):
            par = tb % 2
            P.waits("sync", [ln1_st[tb], xnrd_war[par]])
            t_xn = P.dma("sync", xnrd3[:, par],
                         xn_d[:, tb * 512:(tb + 1) * 512].rearrange("(c p) n -> p c n", p=128))
            if kind == "r":
                tr_tok = None
                for sub in range(4):
                    P.waits("tensor", [msc_done, tr_tok, t_idf])
                    tr_tok = P.op("tensor", lambda e_, e=e, tb=tb, sub=sub: e_.transpose(
                        psS2[0:1, sub * 128:(sub + 1) * 128],
                        msc[:, tb * 4 + sub, e:e + 1], identf[:]), inc="TE")
                P.waits("vector", [tr_tok, rowab_war[0]])
                t_r1 = P.op("vector", lambda e_: e_.tensor_copy(
                    rowAB[0:1, 0:512], psS2[0:1, 0:512]), inc="VE")
                P.wait("tensor", t_r1)
                t_r2r = P.op("tensor", lambda e_: e_.matmul(
                    ps2[:, 0:512], ones_row[:], rowAB[0:1, 0:512],
                    start=True, stop=True), inc="TE")
                rowab_war[0] = t_r2r
                P.wait("vector", t_r2r)
                t_sr = P.op("vector", lambda e_: e_.tensor_copy(
                    s16c[:, 0, :], ps2[:, 0:512]), inc="VE")
                P.waits("vector", [t_xn, xs_war, t_sr])

                def vxs(e_, p=par):
                    r = None
                    for kc in range(NKC):
                        r = e_.tensor_mul(xssb[:, kc, :], xnrd3[:, p, kc, :], s16c[:, 0, :])
                    return r
                t_xs = P.op("vector", vxs, inc="VE")
                rhs_par = None
                rhs_tok = t_xs
            else:
                rhs_par = par
                rhs_tok = t_xn

            def rhs_ap(kc, p=rhs_par, par=par):
                return xssb[:, kc, :] if p is None else xnrd3[:, par, kc, :]

            ev_tok = None
            mul_tok = None
            for Hc in range(NKC):
                P.waits("tensor", [rhs_tok, cur_w1, cur_w3, ev_tok])

                def tf_h(e_, Hc=Hc, f=rhs_ap):
                    for kc in range(NKC):
                        e_.matmul(ps0[:], w1sb[:, kc, Hc * 128:(Hc + 1) * 128], f(kc),
                                  start=(kc == 0), stop=(kc == NKC - 1))
                    r = None
                    for kc in range(NKC):
                        r = e_.matmul(ps1[:], w3sb[:, kc, Hc * 128:(Hc + 1) * 128], f(kc),
                                      start=(kc == 0), stop=(kc == NKC - 1))
                    return r
                t_h = P.op("tensor", tf_h, inc="TE")
                last_h_mm = t_h
                P.waits("scalar", [t_h, mul_tok])

                def sf_h(e_):
                    e_.activation(s16a[:, 0, :], ps0[:], AF.Silu)
                    return e_.activation(s16b[:, 0, :], ps1[:], AF.Copy)
                t_se = P.op("scalar", sf_h, inc="SE")
                ev_tok = t_se
                P.waits("vector", [t_se, hh_war])
                mul_tok = P.op("vector", lambda e_, Hc=Hc: e_.tensor_mul(
                    hhb[:, Hc, :], s16a[:, 0, :], s16b[:, 0, :]), inc="VE")
            hh_ready = mul_tok
            ev_war = [None, None]
            for Dc in range(NKC):
                opar = Dc % 2
                P.waits("tensor", [hh_ready, cur_w2, ev_war[opar]])

                def tf_o3(e_, Dc=Dc, opar=opar):
                    r = None
                    for Hc in range(NKC):
                        r = e_.matmul((ps2 if opar == 0 else ps3)[:],
                                      w2sb[:, Hc, Dc * 128:(Dc + 1) * 128], hhb[:, Hc, :],
                                      start=(Hc == 0), stop=(Hc == NKC - 1))
                    return r
                t_o3 = P.op("tensor", tf_o3, inc="TE")
                last_o_mm = t_o3
                P.waits("vector", [t_o3, s32b_war])
                t_ev = P.op("vector", lambda e_, opar=opar: e_.tensor_copy(
                    s32b[:, opar, :], (ps2 if opar == 0 else ps3)[:]), inc="VE")
                ev_war[opar] = t_ev
                P.wait("gpsimd", t_ev)
                t_acc = P.dma("gpsimd", rx_d[Dc * 128:(Dc + 1) * 128, tb * 512:(tb + 1) * 512],
                              s32b[:, opar, :], accum_op=ALU.add, sem="ACC")
                rx_accum_toks.append(t_acc)
                if opar == 1:
                    s32b_war = t_acc
            xnrd_war[par] = last_h_mm
            xs_war = last_h_mm
            hh_war = last_o_mm
        if ei + 1 < len(expert_list):
            t_w1, t_w3, t_w2 = load_expert_w3(*expert_list[ei + 1], last_h_mm, last_o_mm)
    s3_done_g = rx_accum_toks[-1]
    s3_te = last_o_mm
    if n_stages < 4:
        _finish(P, nc, [s3_done_g, s3_te], debug_taps,
                {"xn0": (xn0_d, [D, T], BF16), "rx": (rx_d, [D, TC], F32),
                 "xn": (xn_d, [D, TC], BF16)})
        with nc.Block() as blk:
            P.run(blk)
        es.close()
        return nc

    ctb = at("ctb", [128, NKC, 512], F32, 36)
    gtb = at("gtb", [128, NKC, 512], F32, 52)
    ybf = at("ybf", [128, NKC, TC], BF16, 68)
    Pbuf = at("Pbuf", [128, NKC, TC], BF16, 100)
    wpph = at("wpph", [128, 2, NKC, 128], BF16, 132)
    wppk = at("wppk", [128, 2, NKC, 128], BF16, 136)
    xn4rd = at("xn4rd", [128, 2, NKC, 512], BF16, 0)
    rxsl4 = at("rxsl4", [128, NKC, 512], F32, 16)
    u4wr = at("u4wr", [128, 2, NKC, 512], BF16, 36)
    agrow = at("agrow", [2, 1024], F32, 152)

    rx4_war = [None, None]
    u4_war = [None, None]
    ln2_st = []
    for tb in range(4):
        par = tb % 2
        P.waits("sync", [s3_done_g, rx4_war[par]])
        t_rx = P.dma("sync", rxsl4[:],
                     rx_d[:, tb * 512:(tb + 1) * 512].rearrange("(c p) n -> p c n", p=128))
        P.waits("scalar", [t_rx, u4_war[par], s3_te, rx4_war[0], rx4_war[1]])

        def scast4(e, p=par):
            r = None
            for kc in range(NKC):
                r = e.activation(u4wr[:, p, kc, :], rxsl4[:, kc, :], AF.Copy)
            return r
        t_cast = P.op("scalar", scast4, inc="SE")
        rx4_war[par] = t_cast
        st_tok = stats_mms(lambda kc, p=par: u4wr[:, p, kc, :], par, sqs, [t_cast])
        t_ab = ln_rows(512, EPS_LN, st_tok, rms=False, par=par)
        P.waits("vector", [t_ab])

        t_app = P.opseq("vector", [
            lambda e, p=par: [e.tensor_mul(u4wr[:, p, kc, :], u4wr[:, p, kc, :],
                                           arep[:, p, 0:512]) for kc in range(NKC)][-1],
            lambda e, p=par: [e.tensor_add(u4wr[:, p, kc, :], u4wr[:, p, kc, :],
                                           brep[:, p, 0:512]) for kc in range(NKC)][-1],
        ], inc="VE")
        u4_war[par] = t_app
        P.wait("sync", t_app)
        t_st = P.dma("sync", xn_d[:, tb * 512:(tb + 1) * 512]
                     .rearrange("(c p) n -> p c n", p=128), u4wr[:, par])
        ln2_st.append(t_st)
    ln2_done = ln2_st[-1]

    wp_war = [None, None]
    xn4_war = [None, None]
    scan_tok = None
    cg_war = s2_done_v
    sg_war = None
    rxld_war = [None, None]
    for tb in range(4):
        par = tb % 2
        P.waits("sync", [ln2_st[tb], xn4_war[par], ln2_done])
        t_xn = P.dma("sync", xn4rd[:, par],
                     xn_d[:, tb * 512:(tb + 1) * 512].rearrange("(c p) n -> p c n", p=128))
        cg_toks = []
        proj_last = None
        for dc in range(NKC):
            wpar = dc % 2
            P.waits("gpsimd", [wp_war[wpar], s3_te])
            t_wh = P.dma("gpsimd", wpph[:, wpar],
                         Wp[:, dc * 128:(dc + 1) * 128].rearrange("(c p) n -> p c n", p=128),
                         sem="WTS")
            t_wk = P.dma("gpsimd", wppk[:, wpar],
                         Wp[:, D + dc * 128:D + (dc + 1) * 128].rearrange("(c p) n -> p c n", p=128),
                         sem="WTS")
            P.waits("tensor", [t_xn, t_wh, t_wk, proj_last, cg_war, sg_war])

            def tf_p4(e, p=par, wpar=wpar):
                for kc in range(NKC):
                    e.matmul(ps0[:], wpph[:, wpar, kc, :], xn4rd[:, p, kc, :],
                             start=(kc == 0), stop=(kc == NKC - 1))
                r = None
                for kc in range(NKC):
                    r = e.matmul(ps1[:], wppk[:, wpar, kc, :], xn4rd[:, p, kc, :],
                                 start=(kc == 0), stop=(kc == NKC - 1))
                return r
            t_pj = P.op("tensor", tf_p4, inc="TE")
            proj_last = t_pj
            wp_war[wpar] = t_pj
            xn4_war[par] = t_pj
            P.waits("scalar", [t_pj, scan_tok])

            def sf_cg(e, dc=dc):
                e.activation(ctb[:, dc, :], ps1[:], AF.Sigmoid, scale=-1.0,
                             bias=bpn_c[:, 8 + dc:8 + dc + 1])
                e.activation(s32a[:, 0, :], ps1[:], AF.Sigmoid,
                             bias=bp_c[:, 8 + dc:8 + dc + 1])
                return e.activation(s32a[:, 1, :], ps0[:], AF.Identity,
                                    bias=bp_c[:, dc:dc + 1])
            t_cg = P.op("scalar", sf_cg, inc="SE")
            P.waits("vector", [t_cg])
            P.waits("scalar", [t_cg])
            t_sg = P.op("scalar", lambda e: e.activation(
                s16a[:, 1, :], s32a[:, 1, :], AF.Sigmoid), inc="SE")
            sg_war = t_cg
            P.waits("vector", [t_sg, cg_war])

            t_g = P.opseq("vector", [
                lambda e: (e.tensor_scalar(s32b[:, 0, :], s32a[:, 1, :], 0.0, None, ALU.is_ge),
                           e.tensor_scalar(s32b[:, 1, :], s32a[:, 1, :], 0.5, None, ALU.add))[-1],
                lambda e: e.tensor_sub(s32b[:, 1, :], s32b[:, 1, :], s16a[:, 1, :]),
                lambda e: e.tensor_mul(s32b[:, 1, :], s32b[:, 1, :], s32b[:, 0, :]),
                lambda e: (e.tensor_add(s16c[:, 1, :], s16a[:, 1, :], s32b[:, 1, :]),
                           e.tensor_copy(s16b[:, 0, :], s32a[:, 0, :]))[-1],
                lambda e, dc=dc: e.tensor_mul(gtb[:, dc, :], s16b[:, 0, :], s16c[:, 1, :]),
            ], inc="VE")
            cg_toks.append(t_g)
            cg_war = t_g
        if tb == 0 and debug_taps and "s4dbg" in debug_taps:
            d_c0 = nc.dram_tensor("dbg_c0", [128, 8 * 512], F32, kind="ExternalOutput").ap()
            d_g0 = nc.dram_tensor("dbg_g0", [128, 8 * 512], F32, kind="ExternalOutput").ap()
            P.waits("sync", cg_toks)
            tc0 = P.dma("sync", d_c0, ctb[:].rearrange("p a b -> p (a b)"))
            tg0 = P.dma("sync", d_g0, gtb[:].rearrange("p a b -> p (a b)"))
            P.waits("vector", [tc0, tg0])
            P.waits("scalar", [tc0, tg0])
        P.waits("vector", cg_toks + [scan_tok])

        for dc in range(NKC):
            tks = P.op("vector", lambda e, dc=dc: e.tensor_tensor_scan(
                s32a[:, 0, :], ctb[:, dc, :], gtb[:, dc, :],
                carry_c[:, dc:dc + 1], ALU.mult, ALU.add), inc="VE")
            P.wait("vector", tks)
            tkc = P.op("vector", lambda e, dc=dc: e.tensor_copy(
                carry_c[:, dc:dc + 1], s32a[:, 0, 511:512]), inc="VE")
            P.wait("vector", tkc)
            tps = P.op("vector", lambda e, dc=dc, tb=tb: e.tensor_tensor_scan(
                Pbuf[:, dc, tb * 512:(tb + 1) * 512], ctb[:, dc, :], zeros_t[:, :],
                pP_c[:, dc:dc + 1], ALU.mult, ALU.add), inc="VE")
            P.wait("vector", tps)
            tpc = P.op("vector", lambda e, dc=dc, tb=tb: e.tensor_copy(
                pP_c[:, dc:dc + 1], Pbuf[:, dc, tb * 512 + 511:tb * 512 + 512]), inc="VE")
            P.wait("vector", tpc)
            scan_tok = P.op("vector", lambda e, dc=dc, tb=tb: e.tensor_copy(
                ybf[:, dc, tb * 512:(tb + 1) * 512], s32a[:, 0, :]), inc="VE")
    scans_done = scan_tok

    P.waits("vector", [scans_done, t_cst])
    t_mask = P.op("vector", lambda e: e.tensor_scalar(
        carry_c[:], carry_c[:], invp_sb[:, 0:1], None, ALU.mult), inc="VE")
    P.wait("gpsimd", t_mask)
    t_agw = P.dma("gpsimd", ag_in, carry_c[:], sem="AGD")
    P.waits("gpsimd", [t_agw])
    t_cc = P.op("gpsimd", lambda e: e.collective_compute(
        "AllGather", ALU.bypass,
        replica_groups=[[0, 1], [2, 3], [4, 5], [6, 7]],
        ins=[ag_in], outs=[ag_out]), inc="CC")
    P.waits("sync", [t_cc])
    t_agr = P.dma("sync", agrow[:], ag_out.rearrange("(r p) ic -> r (p ic)", p=128))
    P.waits("tensor", [t_agr, t_sel])
    t_selmm = P.op("tensor", lambda e: (
        e.matmul(psS2[0:1, 0:512], sel_sb[:], agrow[:, 0:512], start=True, stop=True),
        e.matmul(psS2[0:1, 512:1024], sel_sb[:], agrow[:, 512:1024], start=True, stop=True))[-1],
        inc="TE")
    P.wait("vector", t_selmm)
    t_prow = P.op("vector", lambda e: e.tensor_copy(rowC[0:1, 0:1024], psS2[0:1, 0:1024]),
                  inc="VE")
    P.wait("sync", t_prow)
    t_prd = P.dma("sync", pr_d, rowC[0:1, 0:1024])
    P.waits("sync", [t_prd])
    t_pcar = P.dma("sync", pcar_c[:], pr_d.rearrange("one (p ic) -> p (one ic)", p=128))

    yfix_war = None
    rx_acc4 = []
    s32b_acc_war = [None, None]
    for tb in range(4):
        par = tb % 2
        P.waits("vector", [t_pcar, yfix_war])

        t_fix = None
        for dc in range(NKC):
            t_fix = P.opseq("vector", [
                lambda e, dc=dc, tb=tb: e.tensor_scalar(
                    s16a[:, 0, :], Pbuf[:, dc, tb * 512:(tb + 1) * 512],
                    pcar_c[:, dc:dc + 1], None, ALU.mult),
                lambda e, dc=dc, tb=tb: e.tensor_add(
                    ybf[:, dc, tb * 512:(tb + 1) * 512],
                    ybf[:, dc, tb * 512:(tb + 1) * 512], s16a[:, 0, :]),
            ], inc="VE")
        P.waits("sync", [rxld_war[0], rxld_war[1], t_fix])
        t_rx2 = P.dma("sync", rxsl4[:],
                      rx_d[:, tb * 512:(tb + 1) * 512].rearrange("(c p) n -> p c n", p=128))
        P.waits("vector", [t_rx2])

        def vy(e, tb=tb):
            r = None
            for dc in range(NKC):
                r = e.tensor_add(ybf[:, dc, tb * 512:(tb + 1) * 512],
                                 ybf[:, dc, tb * 512:(tb + 1) * 512],
                                 rxsl4[:, dc, :])
            return r
        t_y = P.op("vector", vy, inc="VE")
        yfix_war = t_y
        st_tok = stats_mms(lambda kc, tb=tb: ybf[:, kc, tb * 512:(tb + 1) * 512], par, sqs,
                           [t_y])
        t_ab = ln_rows(512, EPS_LN1, st_tok, rms=False, par=par)
        P.waits("vector", [t_ab])

        for kc in range(NKC):
            P.waits("vector", [t_ab, s32b_acc_war[kc % 2]])

            t_vk = P.opseq("vector", [
                lambda e, p=par, tb=tb, kc=kc: e.tensor_mul(
                    s16b[:, 0, :], ybf[:, kc, tb * 512:(tb + 1) * 512], arep[:, p, 0:512]),
                lambda e, p=par: e.tensor_add(s16b[:, 0, :], s16b[:, 0, :], brep[:, p, 0:512]),
                lambda e, kc=kc: e.tensor_scalar(s32b[:, kc % 2, :], s16b[:, 0, :],
                                                 gam_c[:, kc:kc + 1], bet_c[:, kc:kc + 1],
                                                 ALU.mult, ALU.add),
            ], inc="VE")
            P.wait("gpsimd", t_vk)
            t_ak = P.dma("gpsimd", rx_d[kc * 128:(kc + 1) * 128, tb * 512:(tb + 1) * 512],
                         s32b[:, kc % 2, :], accum_op=ALU.add, sem="ACC")
            s32b_acc_war[kc % 2] = t_ak
            rx_acc4.append(t_ak)
        rxld_war[par] = t_ak
    s4_done_g = rx_acc4[-1]
    s4_te = t_selmm
    if debug_taps and "s4dbg" in debug_taps:
        d_y = nc.dram_tensor("dbg_y", [128, 8 * TC], BF16, kind="ExternalOutput").ap()
        d_P = nc.dram_tensor("dbg_P", [128, 8 * TC], BF16, kind="ExternalOutput").ap()
        d_car = nc.dram_tensor("dbg_car", [128, 16], F32, kind="ExternalOutput").ap()
        P.waits("sync", [yfix_war, s4_done_g])
        tdy = P.dma("sync", d_y, ybf[:].rearrange("p a b -> p (a b)"))
        P.dma("sync", d_P, Pbuf[:].rearrange("p a b -> p (a b)"))
        P.dma("sync", d_car[:, 0:8], pcar_c[:])
        tdc2 = P.dma("sync", d_car[:, 8:16], carry_c[:])
        P.waits("vector", [tdc2])
        P.waits("scalar", [tdc2])
        P.waits("gpsimd", [tdc2])
        P.waits("tensor", [tdc2])
    if n_stages < 5:
        _finish(P, nc, [s4_done_g, s4_te], debug_taps,
                {"xn0": (xn0_d, [D, T], BF16), "rx": (rx_d, [D, TC], F32),
                 "xn": (xn_d, [D, TC], BF16)})
        with nc.Block() as blk:
            P.run(blk)
        es.close()
        return nc

    w1sb5 = at("w1sb5", [128, NKC, TC], BF16, 36)
    w2sb5 = at("w2sb5", [128, 16, D], BF16, 68)
    hbuf5 = at("hbuf5", [128, 16, 512], BF16, 100)
    ugsl5 = at("ugsl5", [128, 2, NKC, 512], BF16, 116)
    dsb5 = at("dsb5", [128, NKC, 512], F32, 132)
    gr32 = at("gr32", [128, 512], F32, 0)
    fin32 = at("fin32", [128, NKC, 512], F32, 16)

    rx5_war = [None, None]
    u5_war = [None, None]
    u32_war = [None, None]
    ln3_st = []
    dl_st = []
    sc_tok = None
    rms_tok = None
    for tb in range(4):
        par = tb % 2
        P.waits("gpsimd", [s4_done_g, rx5_war[par], s4_te])
        t_rx = P.dma("gpsimd", ugsl5[:, par],
                     rx_d[:, tb * 512:(tb + 1) * 512].rearrange("(c p) n -> p c n", p=128),
                     sem="XS")
        t_cast = t_rx
        st_tok = stats_mms(lambda kc, p=par: ugsl5[:, p, kc, :], par, sqs, [t_cast])
        t_ab = ln_rows(512, EPS_LN, st_tok, rms=False, par=par)
        P.waits("vector", [t_ab])

        t_app = P.opseq("vector", [
            lambda e, p=par: [e.tensor_mul(ugsl5[:, p, kc, :], ugsl5[:, p, kc, :],
                                           arep[:, p, 0:512]) for kc in range(NKC)][-1],
            lambda e, p=par: [e.tensor_add(ugsl5[:, p, kc, :], ugsl5[:, p, kc, :],
                                           brep[:, p, 0:512]) for kc in range(NKC)][-1],
        ], inc="VE")
        u5_war[par] = t_app
        P.wait("sync", t_app)
        t_st = P.dma("sync", xn_d[:, tb * 512:(tb + 1) * 512]
                     .rearrange("(c p) n -> p c n", p=128), ugsl5[:, par])
        ln3_st.append(t_st)
        P.waits("scalar", [t_app, u32_war[par]])

        def s_u32(e, p=par):
            r = None
            for kc in range(NKC):
                r = e.activation(dsb5[:, kc, :], ugsl5[:, p, kc, :], AF.Copy)
            return r
        t_u32 = P.op("scalar", s_u32, inc="SE")
        P.wait("sync", t_u32)
        t_dl = P.dma("sync", dl_d[:, tb * 512:(tb + 1) * 512]
                     .rearrange("(c p) n -> p c n", p=128), dsb5[:])
        dl_st.append(t_dl)
        rx5_war[par] = t_dl
        u32_war[par] = t_dl
        P.waits("tensor", [t_app, t_ce, sc_tok])

        def tf_sc(e, p=par):
            r = None
            for kc in range(NKC):
                r = e.matmul(psS2[0:8, 0:512], cent_sb[:, kc, :], ugsl5[:, p, kc, :],
                             start=(kc == 0), stop=(kc == NKC - 1))
            return r
        t_scm = P.op("tensor", tf_sc, inc="TE")
        P.wait("scalar", t_scm)
        t_sg5 = P.op("scalar", lambda e: e.activation(
            s32a[0:8, 0, 0:512], psS2[0:8, 0:512], AF.Sigmoid), inc="SE")
        sub_tok = None
        for sub in range(4):
            P.waits("tensor", [t_sg5, sub_tok, t_idf])
            t_tr = P.op("tensor", lambda e, sub=sub: e.transpose(
                psS[0:128, 0:8], s32a[0:8, 0, sub * 128:(sub + 1) * 128],
                identf[0:8, 0:8]), inc="TE")
            P.wait("vector", t_tr)

            steps5 = [
                lambda e: e.tensor_copy(s32b[:, 0, 0:8], psS[0:128, 0:8]),
                lambda e: e.reduce_sum(gtmp[:, 1:2], s32b[:, 0, 0:8], AX.X),
                lambda e: e.reciprocal(gtmp[:, 2:3], gtmp[:, 1:2]),
                lambda e: e.reduce_max(gtmp[:, 3:4], s32b[:, 0, 0:8], AX.X),
                lambda e: e.tensor_scalar(s32b[:, 1, 0:8], s32b[:, 0, 0:8], gtmp[:, 3:4], None, ALU.is_lt),
                lambda e: e.tensor_mul(s32b[:, 1, 0:8], s32b[:, 1, 0:8], s32b[:, 0, 0:8]),
                lambda e: e.reduce_max(gtmp[:, 4:5], s32b[:, 1, 0:8], AX.X),
                lambda e: e.tensor_scalar(s32b[:, 1, 0:8], s32b[:, 0, 0:8], gtmp[:, 4:5], None, ALU.is_ge),
                lambda e: e.tensor_mul(s32b[:, 1, 0:8], s32b[:, 1, 0:8], s32b[:, 0, 0:8]),
                lambda e, tb=tb, sub=sub: e.tensor_scalar(gkc[:, tb * 4 + sub, :], s32b[:, 1, 0:8],
                                                          gtmp[:, 2:3], None, ALU.mult),
            ]
            for fstep in steps5:
                sub_tok = P.op("vector", fstep, inc="VE")
                P.wait("vector", sub_tok)
        sc_tok = sub_tok
        st2 = stats_mms(lambda kc, p=par: ugsl5[:, p, kc, :], par, sqs, [t_app], rms=True)
        rms_tok = ln_rows(512, EPS_RMS, st2, rms=True, par=par,
                          save_rows=(rmsrow[0:1, tb * 512:(tb + 1) * 512],))
        rx5_war[par] = rms_tok
    gk_done = sc_tok
    ln3_done = ln3_st[-1]
    rms_done = rms_tok
    dsb_war0 = dl_st[-1]

    ug_st = {}
    u5rd_war = [None, None]
    for grp, (gcol, dst) in enumerate((
            (gsh_c, ug_d), (grt_c, xn_d))):
        toks = []
        for tb in range(4):
            par = tb % 2
            P.waits("sync", [ln3_st[tb], u5rd_war[par]])
            t_u = P.dma("sync", ugsl5[:, par],
                        xn_d[:, tb * 512:(tb + 1) * 512].rearrange("(c p) n -> p c n", p=128))
            src_is_bf = True
            P.waits("vector", [rms_done, t_u, rowab_war[0]])
            t_rb = P.op("vector", lambda e, tb=tb: e.tensor_copy(
                rowAB[0:1, 0:512], rmsrow[0:1, tb * 512:(tb + 1) * 512]), inc="VE")
            P.wait("tensor", t_rb)
            t_rr = P.op("tensor", lambda e: e.matmul(
                ps2[:, 0:512], ones_row[:], rowAB[0:1, 0:512], start=True, stop=True), inc="TE")
            rowab_war[0] = t_rr
            P.wait("vector", t_rr)

            t_ug = P.op("vector", lambda e: e.tensor_copy(s16c[:, 1, :], ps2[:, 0:512]),
                        inc="VE")
            for kc in range(NKC):
                t_ug = P.opseq("vector", [
                    lambda e, p=par, kc=kc: e.tensor_mul(
                        s16b[:, 0, :], ugsl5[:, p, kc, :], s16c[:, 1, :]),
                    lambda e, p=par, kc=kc, gcol=gcol: e.tensor_scalar(
                        ugsl5[:, p, kc, :], s16b[:, 0, :], gcol[:, kc:kc + 1], None, ALU.mult),
                ], inc="VE")
            P.waits("vector", [t_ug])
            P.wait("sync", t_ug)
            t_w = P.dma("sync", dst[:, tb * 512:(tb + 1) * 512]
                        .rearrange("(c p) n -> p c n", p=128), ugsl5[:, par])
            u5rd_war[par] = t_w
            toks.append(t_w)
        ug_st[grp] = toks
    ug_s_done = ug_st[0][-1]
    ug_r_done = ug_st[1][-1]

    vexperts = []
    for e in range(2):
        vexperts.append(("s", e, 0))
        vexperts.append(("s", e, 1))
    for e in range(8):
        vexperts.append(("r", e, 0))
        vexperts.append(("r", e, 1))

    def load_vw(kind, e, hh, after_h, after_o):
        W1x = W1s_i if kind == "s" else W1r_i
        W2x = W2s_i if kind == "s" else W2r_i
        P.waits("gpsimd", [after_h, s4_te])
        a = P.dma("gpsimd", w1sb5[:],
                  W1x[e][:, hh * TC:(hh + 1) * TC].rearrange("(c p) n -> p c n", p=128),
                  sem="WTS")
        P.waits("gpsimd", [after_o])
        b = P.dma("gpsimd", w2sb5[:],
                  W2x[e][hh * TC:(hh + 1) * TC, :].rearrange("(c p) n -> p c n", p=128),
                  sem="WTS")
        return a, b

    tw1, tw2 = load_vw(*vexperts[0], None, None)
    ug5_war = [None, None]
    h5_war = None
    dsb_war = dsb_war0
    last_h5 = None
    last_o5 = None
    dl_acc = []
    for vi, (kind, e, hh) in enumerate(vexperts):
        cw1, cw2 = tw1, tw2
        src_d = ug_d if kind == "s" else xn_d
        src_rdy = ug_s_done if kind == "s" else ug_r_done
        bcol = (b1s_c if kind == "s" else b1r_c)
        b2col = (b2s_c if kind == "s" else b2r_c)
        for tp in range(4):
            par = tp % 2
            P.waits("sync", [src_rdy, ug5_war[par]])
            t_u = P.dma("sync", ugsl5[:, par],
                        src_d[:, tp * 512:(tp + 1) * 512].rearrange("(c p) n -> p c n", p=128))
            if kind == "r":
                sub_tok2 = None
                for sub in range(4):
                    P.waits("tensor", [gk_done, sub_tok2, t_idf])
                    sub_tok2 = P.op("tensor", lambda e_, e=e, tp=tp, sub=sub: e_.transpose(
                        psS2[0:1, sub * 128:(sub + 1) * 128],
                        gkc[:, tp * 4 + sub, e:e + 1], identf[:]), inc="TE")
                P.waits("vector", [sub_tok2, rowab_war[0]])
                t_gr1 = P.op("vector", lambda e_: e_.tensor_copy(
                    rowAB[0:1, 0:512], psS2[0:1, 0:512]), inc="VE")
                P.wait("tensor", t_gr1)
                t_gr2 = P.op("tensor", lambda e_: e_.matmul(
                    ps2[:, 0:512], ones_row[:], rowAB[0:1, 0:512], start=True, stop=True),
                    inc="TE")
                rowab_war[0] = t_gr2
                P.wait("vector", t_gr2)
                t_gr = P.op("vector", lambda e_: e_.tensor_copy(gr32[:, :], ps2[:, 0:512]),
                            inc="VE")
            else:
                t_gr = None
            ev5 = None
            ev5_hist = [None, None]
            for Hc in range(16):
                P.waits("tensor", [t_u, cw1, ev5_hist[Hc % 2]])

                def tf_h5(e_, Hc=Hc, p=par):
                    r = None
                    for kc in range(NKC):
                        r = e_.matmul(ps0[:] if Hc % 2 == 0 else ps1[:],
                                      w1sb5[:, kc, Hc * 128:(Hc + 1) * 128],
                                      ugsl5[:, p, kc, :],
                                      start=(kc == 0), stop=(kc == NKC - 1))
                    return r
                t_h5 = P.op("tensor", tf_h5, inc="TE")
                last_h5 = t_h5
                P.waits("scalar", [t_h5, h5_war])
                ev5 = P.op("scalar", lambda e_, Hc=Hc, e=e, hh=hh, bcol=bcol: e_.activation(
                    hbuf5[:, Hc, :], (ps0 if Hc % 2 == 0 else ps1)[:], AF.Gelu,
                    bias=bcol[:, e, hh * 16 + Hc:hh * 16 + Hc + 1]), inc="SE")
                ev5_hist[Hc % 2] = ev5
            h5_done = ev5
            ep_war = [None, None]
            for Dc in range(NKC):
                opar = Dc % 2
                P.waits("tensor", [h5_done, cw2, ep_war[opar]])

                def tf_o5(e_, Dc=Dc, opar=opar):
                    r = None
                    for Hc in range(16):
                        r = e_.matmul((ps2 if opar == 0 else ps3)[:],
                                      w2sb5[:, Hc, Dc * 128:(Dc + 1) * 128], hbuf5[:, Hc, :],
                                      start=(Hc == 0), stop=(Hc == 15))
                    return r
                t_o5 = P.op("tensor", tf_o5, inc="TE")
                last_o5 = t_o5
                P.waits("vector", [t_o5, dsb_war] + ([t_gr] if kind == "r" else []))
                if kind == "r":
                    def v_ep(e_, Dc=Dc, opar=opar, e=e, hh=hh, b2col=b2col):
                        if hh == 0:
                            return e_.scalar_tensor_tensor(
                                dsb5[:, Dc, :], (ps2 if opar == 0 else ps3)[:],
                                b2col[:, e, Dc:Dc + 1], gr32[:, :], ALU.add, ALU.mult)
                        return e_.scalar_tensor_tensor(
                            dsb5[:, Dc, :], (ps2 if opar == 0 else ps3)[:],
                            0.0, gr32[:, :], ALU.add, ALU.mult)
                else:
                    def v_ep(e_, Dc=Dc, opar=opar, e=e, hh=hh, b2col=b2col):
                        if hh == 0:
                            return e_.tensor_scalar(
                                dsb5[:, Dc, :], (ps2 if opar == 0 else ps3)[:],
                                b2col[:, e, Dc:Dc + 1], None, ALU.add)
                        return e_.tensor_copy(dsb5[:, Dc, :], (ps2 if opar == 0 else ps3)[:])
                t_ep = P.op("vector", v_ep, inc="VE")
                ep_war[opar] = t_ep
            h5_war = last_o5
            P.wait("gpsimd", t_ep)
            t_dacc = P.dma("gpsimd", dl_d[:, tp * 512:(tp + 1) * 512]
                           .rearrange("(c p) n -> p c n", p=128), dsb5[:],
                           accum_op=ALU.add, sem="ACC")
            dl_acc.append(t_dacc)
            dsb_war = t_dacc
            ug5_war[par] = last_h5
        if vi + 1 < len(vexperts):
            tw1, tw2 = load_vw(*vexperts[vi + 1], last_h5, last_o5)
    dl_done = dl_acc[-1]

    fin_war = None
    out_toks = []
    for tb in range(4):
        P.waits("sync", [dl_done, s4_done_g, fin_war])
        t_r = P.dma("sync", fin32[:],
                    rx_d[:, tb * 512:(tb + 1) * 512].rearrange("(c p) n -> p c n", p=128))
        t_d = P.dma("sync", dsb5[:],
                    dl_d[:, tb * 512:(tb + 1) * 512].rearrange("(c p) n -> p c n", p=128))
        P.waits("vector", [t_r, t_d])

        def v_fin(e):
            r = None
            for kc in range(NKC):
                r = e.tensor_add(fin32[:, kc, :], fin32[:, kc, :], dsb5[:, kc, :])
            return r
        t_f = P.op("vector", v_fin, inc="VE")
        P.wait("sync", t_f)
        t_o = P.dma("sync", out[:, tb * 512:(tb + 1) * 512]
                    .rearrange("(c p) n -> p c n", p=128), fin32[:])
        out_toks.append(t_o)
        fin_war = t_o
    _finish(P, nc, out_toks, debug_taps,
            {"xn0": (xn0_d, [D, T], BF16), "rx": (rx_d, [D, TC], F32),
             "dl": (dl_d, [D, TC], F32), "xn": (xn_d, [D, TC], BF16),
             "ug": (ug_d, [D, TC], BF16)})
    with nc.Block() as blk:
        P.run(blk)
    es.close()
    return nc


_CACHE = {}


def _host_tables():
    f32 = np.float32
    freqs = (10000.0 ** (-np.arange(0, DH, 2, dtype=f32) / DH)).astype(f32)
    pos = np.arange(T, dtype=f32)[:, None]
    ang = (pos * freqs).astype(f32)
    s = np.sin(ang).astype(f32)
    c = np.cos(ang).astype(f32)
    s = np.stack([s, s], axis=-1).reshape(T, DH)
    c = np.stack([c, c], axis=-1).reshape(T, DH)
    ff = np.arange(1, NF + 1, dtype=f32)
    fourier = np.concatenate([np.sin(pos * ff), np.cos(pos * ff)], axis=-1).astype(f32)
    cosT = np.ascontiguousarray(np.concatenate([c.T, c.T], axis=0))
    sinT = np.ascontiguousarray(np.concatenate([s.T, s.T], axis=0))
    fourT = np.ascontiguousarray(fourier.T)
    R = np.zeros((DH, DH), np.float32)
    for r in range(32):
        R[r, 2 * r + 1] = -1.0
    for r in range(32, DH):
        R[r, 2 * (r - 32)] = 1.0
    R2 = np.zeros((128, 128), np.float32)
    R2[0:64, 0:64] = R.T
    R2[64:128, 64:128] = R.T
    return cosT, sinT, fourT, R2


def kernel(**inp):
    if "nc" not in _CACHE:
        _CACHE["nc"] = build_program()
    nc = _CACHE["nc"]
    f32 = np.float32
    cosT, sinT, fourT, R2 = _host_tables()

    def C(a):
        return np.ascontiguousarray(a.astype(f32, copy=False))

    shared = dict(
        wqkvT=C(inp["L0_attn_w_qkv"].T),
        wfT=C(inp["L0_attn_w_fproj"].T),
        bfp=C(np.tile(inp["L0_attn_b_fproj"], 2).reshape(1, 128).T),
        woutT=C(inp["L0_attn_w_out"].T),
        bout=C(inp["L0_attn_b_out"].reshape(8, 128).T),
        gwT=C(inp["L0_ffn_gate_w"].T),
        w1=C(inp["L0_ffn_w1"]), w2=C(inp["L0_ffn_w2"]), w3=C(inp["L0_ffn_w3"]),
        w1s=C(inp["L0_ffn_w1s"][0]), w2s=C(inp["L0_ffn_w2s"][0]), w3s=C(inp["L0_ffn_w3s"][0]),
        Wp=C(inp["L1_attn_W"]),
        bp=C(inp["L1_attn_b"].reshape(16, 128).T),
        gam=C(inp["L1_attn_gamma"].reshape(8, 128).T),
        bet=C(inp["L1_attn_beta"].reshape(8, 128).T),
        gsh=C(inp["L1_ffn_g_shared"].reshape(8, 128).T),
        grt=C(inp["L1_ffn_g_routed"].reshape(8, 128).T),
        W1s=C(inp["L1_ffn_W1_s"]), b1s=C(inp["L1_ffn_b1_s"].reshape(2, 32, 128).transpose(2, 0, 1)),
        W2s=C(inp["L1_ffn_W2_s"]), b2s=C(inp["L1_ffn_b2_s"].reshape(2, 8, 128).transpose(2, 0, 1)),
        W1r=C(inp["L1_ffn_W1_r"]), b1r=C(inp["L1_ffn_b1_r"].reshape(8, 32, 128).transpose(2, 0, 1)),
        W2r=C(inp["L1_ffn_W2_r"]), b2r=C(inp["L1_ffn_b2_r"].reshape(8, 8, 128).transpose(2, 0, 1)),
        cent=C(inp["L1_ffn_centroids"]),
        R2=R2,
    )
    x = inp["L0_attn_x"].astype(f32, copy=False)
    in_maps = []
    for c in range(8):
        b, h = c // 2, c % 2
        own = np.arange(h * TC, (h + 1) * TC)
        oth = np.arange((1 - h) * TC, (2 - h) * TC)
        perm = np.concatenate([own, oth])
        m = dict(shared)
        m["xT"] = np.ascontiguousarray(x[b].T[:, perm])
        m["cosT"] = np.ascontiguousarray(cosT[:, perm])
        m["sinT"] = np.ascontiguousarray(sinT[:, perm])
        m["fourT"] = np.ascontiguousarray(fourT[:, perm])
        m["invpar"] = np.full((128, 1), 1.0 - h, dtype=f32)
        s = np.zeros((2, 1), f32)
        s[1 - h, 0] = 1.0
        m["sel2"] = s
        in_maps.append(m)

    res = run_bass_kernel_spmd(nc, in_maps, core_ids=list(range(8)))
    outv = np.empty((4, T, D), np.float32)
    for c in range(8):
        b, h = c // 2, c % 2
        outv[b, h * TC:(h + 1) * TC, :] = res.results[c]["out"].T
    return outv


if __name__ == "__main__":
    build_program()
